# version 20
# baseline (speedup 1.0000x reference)
import math
from contextlib import ExitStack

import numpy as np

import concourse.bass as bass
import concourse.mybir as mybir

F32 = mybir.dt.float32
BF16 = mybir.dt.bfloat16
AF = mybir.ActivationFunctionType
ALU = mybir.AluOpType
AX = mybir.AxisListType

D = 1024
EPS_LN = 1e-5
EPS_RMS = 1e-6
NEG_LN8 = -math.log(8.0)


class Trk:

    def __init__(self):
        self.w = None
        self.rs = []

    def wdeps(self):
        return [self.w] + self.rs

    def wrote(self, mark):
        self.w = mark
        self.rs = []

    def rdeps(self):
        return [self.w]

    def read(self, mark):
        self.rs.append(mark)


class Ctx:
    ENG = ("tensor", "vector", "scalar", "sync", "gpsimd")
    SEM_OF_ENG = {"T": "tensor"}

    def __init__(self, nc, stack):
        self.nc = nc
        self.streams = {e: [] for e in self.ENG}
        self.sems = {}
        self.cnt = {}
        self.last_wait = {e: {} for e in self.ENG}
        self.pending = {e: [] for e in self.ENG}
        for s in ("T", "V", "S", "Qs", "Qg"):
            self.sems[s] = stack.enter_context(nc.semaphore(s))
            self.cnt[s] = 0

    def op(self, eng, fn, deps=(), inc=None, amt=1):
        deps = list(deps) + self.pending[eng]
        self.pending[eng] = []
        dd = {}
        for d in deps:
            if d is None or self.SEM_OF_ENG.get(d[0]) == eng:
                continue
            dd[d[0]] = max(dd.get(d[0], 0), d[1])
        deps = tuple(sorted(dd.items()))
        mark = None
        if inc is not None:
            self.cnt[inc] += amt
            mark = (inc, self.cnt[inc])
        lw = self.last_wait[eng]
        sems = self.sems

        def emit(e, _fn=fn, _deps=deps, _inc=inc, _amt=amt):
            for s, c in _deps:
                if lw.get(s, 0) < c:
                    e.wait_ge(sems[s], c)
                    lw[s] = c
            inst = _fn(e)
            if _inc is not None and inst is not None:
                inst.then_inc(sems[_inc], _amt)

        self.streams[eng].append(emit)
        return mark

    def t(self, fn, deps=()):
        return self.op("tensor", fn, deps, "T")

    def v(self, fn, deps=()):
        return self.op("vector", fn, deps, "V")

    def s(self, fn, deps=()):
        return self.op("scalar", fn, deps, "S")

    def dma(self, fn, deps=(), q="Qs"):
        eng = "sync" if q == "Qs" else "gpsimd"
        return self.op(eng, fn, deps, q, 16)

    def barrier(self):
        marks = [(s, self.cnt[s]) for s in ("T", "V", "S", "Qs", "Qg")]
        for e in self.ENG:
            self.pending[e] = list(marks)

    def finalize(self, final_deps):
        self.op("sync", lambda e: None, deps=final_deps)
        nc = self.nc
        with nc.Block() as block:
            @block.tensor
            def _(t):
                for f in self.streams["tensor"]:
                    f(t)

            @block.vector
            def _(v):
                for f in self.streams["vector"]:
                    f(v)

            @block.scalar
            def _(s):
                for f in self.streams["scalar"]:
                    f(s)

            @block.sync
            def _(s):
                for f in self.streams["sync"]:
                    f(s)

            @block.gpsimd
            def _(g):
                for f in self.streams["gpsimd"]:
                    f(g)


def build(T, dbg=False, route_terms=(1,1,1,1)):
    NT = T // 128
    NC = T // 512
    nc = bass.Bass("TRN2", target_bir_lowering=False, debug=False)
    st = ExitStack()
    ctx = Ctx(nc, st)

    dr = lambda name, shape, dt, kind: nc.dram_tensor(name, shape, dt, kind=kind).ap()
    I, O, N_ = "ExternalInput", "ExternalOutput", "Internal"

    x = dr("x", [T, D], F32, I)
    Wq = dr("Wq", [D, D], F32, I)
    Wkv = dr("Wkv", [D, 128], F32, I)
    Wk = dr("Wk", [128, 512], F32, I)
    WkTd = dr("WkT", [128, 16 * 128], F32, I)
    Wv = dr("Wv", [128, 512], F32, I)
    WoS = dr("WoS", [512, D], F32, I)
    qg = dr("qg", [128, 64], F32, I)
    kg = dr("kg", [128, 32], F32, I)
    nwT = dr("nwT", [D, 48], F32, I)
    nbrow = dr("nbrow", [1, 48], F32, I)
    Wff = {f: dict(
        w1=dr(f"W1{f}", [D, D], F32, I), w2=dr(f"W2{f}", [D, D], F32, I),
        b1=dr(f"b1{f}", [128, 8], F32, I), b2=dr(f"b2{f}", [16, D], F32, I))
        for f in "qkv"}
    idbf = dr("idbf", [128, 128], BF16, I)
    idf = dr("idf", [128, 128], F32, I)
    ones1f = dr("ones1f", [1, 128], F32, I)
    esel = dr("esel", [16, 1024], BF16, I)
    cmask = dr("cmask", [128, 1920], BF16, I)
    hsel = dr("hsel", [16, 4 * 128], F32, I)

    out = dr("out", [T, D], F32, O)
    x2h = dr("x2h", [T, D], F32, N_)
    x3hiTh = dr("x3hiTh", [D, T], BF16, N_)
    qTh = dr("qTh", [D, T], BF16, N_)
    hfh = {f: dr(f"h{f}h", [D, T], BF16, N_) for f in "qkv"}
    if dbg:
        dbg_x1 = dr("dbg_x1", [T, D], BF16, O)
        dbg_qkT = dr("dbg_qkT", [128, 4 * T], BF16, O)
        dbg_vc = dr("dbg_vc", [128, 528 * (T // 128)], BF16, O)
        dbg_rms8 = dr("dbg_rms8", [128, 16 * (T // 128)], F32, O)
        dbg_x2 = dr("dbg_x2", [T, D], F32, O)
        dbg_ocn = dr("dbg_ocn", [128, 4 * 1024], BF16, O)
        dbg_qtl = dr("dbg_qtl", [128, 2 * 1024], BF16, O)
        dbg_S = dr("dbg_S", [128, NT * 48], F32, O)
        dbg_hiT = dr("dbg_hiT", [D, T], BF16, O)
        dbg_sel = dr("dbg_sel", [128, NT * 48], F32, O)
        dbg_qT = dr("dbg_qT", [D, T], BF16, O)
        dbg_kTb = dr("dbg_kTb", [D, T], BF16, O)
        dbg_v = dr("dbg_v", [128, NT * 1024], BF16, O)

    sb = lambda name, shape, dt: st.enter_context(nc.sbuf_tensor(name, shape, dt))

    idbf_s = sb("idbf_s", [128, 128], BF16)
    idf_s = sb("idf_s", [128, 128], F32)
    ones1f_s = sb("ones1f_s", [1, 128], F32)
    ones1b_s = sb("ones1b_s", [1, 128], BF16)
    esel_s = sb("esel_s", [16, 1024], BF16)
    stats = sb("stats", [128, 128], F32)
    bnst = sb("bnst", [128, 24], F32)
    cb = sb("cb", [128, 4], F32)
    S_all = sb("S_all", [128, 32, 48], F32)
    mc = []
    mc.append(ctx.dma(lambda e: e.dma_start(out=idbf_s[:], in_=idbf[:, :]), q="Qg"))
    mc.append(ctx.dma(lambda e: e.dma_start(out=idf_s[:], in_=idf[:, :]), q="Qg"))
    mc.append(ctx.dma(lambda e: e.dma_start(out=ones1f_s[:], in_=ones1f[:, :]), q="Qg"))
    mc.append(ctx.dma(lambda e: e.dma_start(out=esel_s[:], in_=esel[:, :]), q="Qg"))
    m_ones_b = ctx.v(lambda e: e.tensor_copy(ones1b_s[:], ones1f_s[:]), deps=[mc[2]])
    ctx.v(lambda e: e.memset(cb[:, 0:1], EPS_LN))
    ctx.v(lambda e: e.memset(cb[:, 1:2], EPS_RMS))
    MCB = ctx.v(lambda e: e.memset(cb[:, 2:3], NEG_LN8))
    MC = ("Qg", ctx.cnt["Qg"])

    tk_sq = Trk()
    tk_ln = [Trk(), Trk()]
    tk_stq = Trk()
    tk_stk = Trk()

    def layer_norm(xap, out_ap, deps_in, par=0, n=D):
        bb = par * 12
        sb_ = 0 if par == 0 else 8
        deps_in = list(deps_in)
        b1 = ctx.v(lambda e: e.bn_stats(bnst[:, bb:bb + 6], xap[:, 0:512]),
                   deps=deps_in + tk_ln[par].wdeps())
        b2 = ctx.v(lambda e: e.bn_stats(bnst[:, bb + 6:bb + 12], xap[:, 512:1024]),
                   deps=deps_in)
        ag = ctx.v(lambda e: e.bn_aggr(stats[:, sb_:sb_ + 2], bnst[:, bb:bb + 12]),
                   deps=[b1, b2])
        ml = ctx.s(lambda e: e.activation(stats[:, sb_ + 2:sb_ + 3], stats[:, sb_ + 1:sb_ + 2],
                                          AF.Ln, bias=cb[:, 0:1]), deps=[ag, MCB])
        mr = ctx.s(lambda e: e.activation(stats[:, sb_ + 3:sb_ + 4], stats[:, sb_ + 2:sb_ + 3],
                                          AF.Exp, scale=-0.5), deps=[ml])
        mn = ctx.v(lambda e: e.tensor_scalar(stats[:, sb_ + 4:sb_ + 5], stats[:, sb_:sb_ + 1],
                                             stats[:, sb_ + 3:sb_ + 4], -1.0,
                                             ALU.mult, ALU.mult), deps=[mr, ag])
        mo = ctx.v(lambda e: e.tensor_scalar(out_ap, xap, stats[:, sb_ + 3:sb_ + 4],
                                             stats[:, sb_ + 4:sb_ + 5],
                                             ALU.mult, ALU.add), deps=[mr, mn] + deps_in)
        tk_ln[par].wrote(mo)
        return mo

    stA = ExitStack()
    sbA = lambda name, shape, dt: stA.enter_context(nc.sbuf_tensor(name, shape, dt))
    psA = lambda name, shape, dt: stA.enter_context(nc.psum_tensor(name, shape, dt))

    cmask_s = sbA("cmask_s", [128, 1920], BF16)
    hsel_s = sbA("hsel_s", [16, 4, 128], F32)
    dens = sbA("dens", [16, 512], F32)
    recs = sbA("recs", [16, 512], F32)
    denrow = sbA("denrow", [1, 2, 512], F32)
    nw_hi = sbA("nw_hi", [128, 8, 48], BF16)
    nw_lo = sbA("nw_lo", [128, 8, 48], BF16)
    nb_hi = sbA("nb_hi", [1, 48], BF16)
    nb_lo = sbA("nb_lo", [1, 48], BF16)
    mc.append(ctx.dma(lambda e: e.dma_start(out=cmask_s[:], in_=cmask[:, :]), q="Qg"))
    MCM = mc[-1]
    mc.append(ctx.dma(lambda e: e.dma_start(
        out=hsel_s[:].rearrange("p g n -> p (g n)"), in_=hsel[:, :]), q="Qg"))
    MHS = mc[-1]
    sq_big = sbA("sq_big", [128, 2, 512], F32)
    qkT = sbA("qkT", [128, 4, T], BF16)
    kvTf = sbA("kvTf", [128, T], BF16)
    wkT_s = sbA("wkT_s", [128, 16, 128], BF16)
    qtlA = sbA("qtlA", [128, 512], BF16)
    qtlB = sbA("qtlB", [128, 512], BF16)
    vc33 = sbA("vc33", [128, NT, 528], BF16)
    rms8 = sbA("rms8", [128, NT, 16], F32)
    wo_s = sbA("wo_s", [128, 4, 1024], BF16)
    qg_s = sbA("qg_s", [128, 64], F32)
    kg_s = sbA("kg_s", [128, 32], F32)
    qkg_s = sbA("qkg_s", [128, 32], F32)
    xin = sbA("xin", [128, 2, D], F32)
    expT = sbA("expT", [128, 6, 512], BF16)
    ocnT = sbA("ocnT", [128, 4, 512], BF16)
    x2s = sbA("x2s", [128, 2, D], F32)
    x3s = sbA("x3s", [128, 2, D], F32)
    lnt = sbA("lnt", [128, 2, D], F32)
    x3Th = sbA("x3Th", [128, 2, 8, 128], BF16)
    x3Tl = sbA("x3Tl", [128, 2, 8, 128], BF16)
    wq_s = sbA("wq_s", [128, 8, 1024], BF16)
    wkv_s = sbA("wkv_s", [128, 8, 128], BF16)
    wk_s = sbA("wk_s", [128, 512], BF16)
    wv_s = sbA("wv_s", [128, 512], BF16)
    nw_f = sbA("nw_f", [128, 8, 48], F32)
    nb_f = sbA("nb_f", [1, 48], F32)
    x1bf = sbA("x1bf", [128, 2, D], BF16)
    x1T = sbA("x1T", [128, 2, 8, 128], BF16)
    kvbf = sbA("kvbf", [128, 2, 128], BF16)

    for g in range(8):
        ctx.dma(lambda e, g=g: e.dma_start(
            out=wq_s[:, g, :], in_=Wq[g * 128:(g + 1) * 128, :]), q="Qg")
        ctx.dma(lambda e, g=g: e.dma_start(
            out=wkv_s[:, g, :], in_=Wkv[g * 128:(g + 1) * 128, :]), q="Qg")
        ctx.dma(lambda e, g=g: e.dma_start(
            out=nw_f[:, g, :], in_=nwT[g * 128:(g + 1) * 128, :]), q="Qg")
    ctx.dma(lambda e: e.dma_start(out=wk_s[:], in_=Wk[:, :]), q="Qg")
    ctx.dma(lambda e: e.dma_start(out=wv_s[:], in_=Wv[:, :]), q="Qg")
    ctx.dma(lambda e: e.dma_start(out=nb_f[:], in_=nbrow[:, :]), q="Qg")
    for g in range(4):
        ctx.dma(lambda e, g=g: e.dma_start(
            out=wo_s[:, g, :], in_=WoS[g * 128:(g + 1) * 128, :]), q="Qg")
    ctx.dma(lambda e: e.dma_start(out=qg_s[:], in_=qg[:, :]), q="Qg")
    ctx.dma(lambda e: e.dma_start(out=kg_s[:], in_=kg[:, :]), q="Qg")
    ctx.dma(lambda e: e.dma_start(
        out=wkT_s[:].rearrange("p h l -> p (h l)"), in_=WkTd[:, :]), q="Qg")
    MW = ("Qg", ctx.cnt["Qg"])
    m_qkg = ctx.v(lambda e: e.tensor_tensor(qkg_s[:], qg_s[:, 0:32], kg_s[:], ALU.mult),
                  deps=[MW])
    m_vc1 = ctx.v(lambda e: e.memset(
        vc33[:].rearrange("p t (h j) -> p t h j", j=33)[:, :, :, 32], 1.0))
    mhi_c = ctx.v(lambda e: e.tensor_copy(
        nw_hi[:].rearrange("p g n -> p (g n)"), nw_f[:].rearrange("p g n -> p (g n)")),
        deps=[MW])
    mlo_c = ctx.v(lambda e: e.tensor_tensor(
        nw_lo[:].rearrange("p g n -> p (g n)"), nw_f[:].rearrange("p g n -> p (g n)"),
        nw_hi[:].rearrange("p g n -> p (g n)"), ALU.subtract), deps=[mhi_c])
    mnbh = ctx.v(lambda e: e.tensor_copy(nb_hi[:], nb_f[:]), deps=[MW])
    mnbl = ctx.v(lambda e: e.tensor_tensor(nb_lo[:], nb_f[:], nb_hi[:], ALU.subtract),
                 deps=[mnbh])
    MNW = [mhi_c, mlo_c, mnbh, mnbl, m_ones_b]

    pq = psA("pq", [128, 512], F32)
    pkvc = psA("pkvc", [128, 512], F32)
    pT = psA("pT", [128, 1024], BF16)
    pscore = psA("pscore", [128, 3, 512], F32)
    pav = psA("pav", [128, 512], F32)
    paux = psA("paux", [128, 512], F32)

    t_xin = [Trk(), Trk()]
    t_x1bf = [Trk(), Trk()]
    t_x1T = [Trk(), Trk()]
    t_kvbf = [Trk(), Trk()]
    t_pq = Trk()
    t_pkvc = Trk()
    t_pT = Trk()
    t_kvT = Trk()
    t_vc33 = Trk()
    t_qkT = Trk()
    t_rms8 = Trk()
    tk_rmsq = [Trk(), Trk()]
    tk_rmsk = [Trk(), Trk()]
    tk_sqh = [Trk(), Trk()]
    rms8m = [None] * NT

    def emit_a0_tile_gen(i):
        r = i % 2
        mload = ctx.dma(lambda e, i=i, r=r: e.dma_start(
            out=xin[:, r, :], in_=x[i * 128:(i + 1) * 128, :]), deps=t_xin[r].wdeps())
        t_xin[r].wrote(mload)
        m_x1 = layer_norm(xin[:, r, :], x1bf[:, r, :],
                          [mload] + t_x1bf[r].wdeps(), par=r)
        t_xin[r].read(m_x1)
        t_x1bf[r].wrote(m_x1)
        yield
        mtr = None
        for g in range(8):
            mtr = ctx.t(lambda e, g=g, r=r: e.transpose(
                pT[:, g * 128:(g + 1) * 128], x1bf[:, r, g * 128:(g + 1) * 128],
                idbf_s[:]),
                deps=[m_x1, mc[0]] + (t_pT.wdeps() if g == 0 else []))
        t_pT.wrote(mtr)
        m_x1T = ctx.v(lambda e, r=r: e.tensor_copy(
            x1T[:, r].rearrange("p g t -> p (g t)"), pT[:]),
            deps=[mtr] + t_x1T[r].wdeps())
        t_pT.read(m_x1T)
        t_x1bf[r].read(m_x1T)
        t_x1T[r].wrote(m_x1T)
        yield
        mq_done = [None, None]
        for half in range(2):
            mq_mm = None
            for g in range(8):
                mq_mm = ctx.t(lambda e, g=g, r=r, half=half: e.matmul(
                    pq[:], x1T[:, r, g, :],
                    wq_s[:, g, half * 512:(half + 1) * 512],
                    start=(g == 0), stop=(g == 7), skip_group_check=True),
                    deps=[m_x1T, MW] + (t_pq.wdeps() if g == 0 else []))
            t_pq.wrote(mq_mm)
            yield
            mSq = ctx.s(lambda e, r=r: e.activation(
                sq_big[:, r, :], pq[:], AF.Square),
                deps=[mq_mm] + tk_sqh[r].wdeps())
            tk_sqh[r].wrote(mSq)
            mred = ctx.v(lambda e, r=r, half=half: e.reduce_sum(
                stats[:, 16 + r * 16 + half * 8:24 + r * 16 + half * 8],
                sq_big[:, r].rearrange("p (h j) -> p h j", j=64), axis=AX.X),
                deps=[mSq] + (tk_rmsq[r].wdeps() if half == 0 else []))
            tk_sqh[r].read(mred)
            mln = ctx.s(lambda e, r=r, half=half: e.activation(
                stats[:, 16 + r * 16 + half * 8:24 + r * 16 + half * 8],
                stats[:, 16 + r * 16 + half * 8:24 + r * 16 + half * 8],
                AF.Ln, scale=1.0 / 64, bias=cb[:, 1:2]), deps=[mred, MCB])
            mrs = ctx.s(lambda e, r=r, half=half: e.activation(
                stats[:, 48 + r * 16 + half * 8:56 + r * 16 + half * 8],
                stats[:, 16 + r * 16 + half * 8:24 + r * 16 + half * 8],
                AF.Exp, scale=-0.5), deps=[mln])
            tk_rmsq[r].wrote(mrs)
            mscl = ctx.v(lambda e, r=r, half=half: e.tensor_tensor(
                sq_big[:, r, half * 256:(half + 1) * 256].rearrange(
                    "p (h j) -> p h j", j=32),
                stats[:, 48 + r * 16 + half * 8:56 + r * 16 + half * 8].rearrange(
                    "p (h o) -> p h o", o=1).broadcast_to([128, 8, 32]),
                qkg_s[:].rearrange("p (o j) -> p o j", o=1).broadcast_to([128, 8, 32]),
                ALU.mult), deps=[mrs, m_qkg] + tk_sqh[r].wdeps())
            tk_sqh[r].wrote(mscl)
            tk_rmsq[r].read(mscl)
            mq = ctx.v(lambda e, r=r, half=half: e.tensor_tensor(
                x1bf[:, r, half * 256:(half + 1) * 256].rearrange(
                    "p (h j) -> p h j", j=32),
                pq[:].rearrange("p (h j) -> p h j", j=64)[:, :, 0:32],
                sq_big[:, r, half * 256:(half + 1) * 256].rearrange(
                    "p (h j) -> p h j", j=32),
                ALU.mult), deps=[mscl] + t_x1bf[r].wdeps())
            t_x1bf[r].wrote(mq)
            t_pq.read(mq)
            tk_sqh[r].read(mq)
            mq_done[half] = mq
            yield
        mtr2 = None
        for g in range(4):
            mtr2 = ctx.t(lambda e, g=g, r=r: e.transpose(
                pT[:, g * 128:(g + 1) * 128], x1bf[:, r, g * 128:(g + 1) * 128],
                idbf_s[:]), deps=mq_done + (t_pT.wdeps() if g == 0 else []))
        t_pT.wrote(mtr2)
        m_qkT = ctx.v(lambda e, i=i: e.tensor_copy(
            qkT[:, :, i * 128:(i + 1) * 128],
            pT[:, 0:512].rearrange("p (g t) -> p g t", g=4)), deps=[mtr2])
        t_pT.read(m_qkT)
        t_x1bf[r].read(m_qkT)
        t_qkT.wrote(m_qkT)
        yield
        mkv_mm = None
        for g in range(8):
            mkv_mm = ctx.t(lambda e, g=g, r=r: e.matmul(
                pkvc[:, 0:128], x1T[:, r, g, :], wkv_s[:, g, :],
                start=(g == 0), stop=(g == 7), skip_group_check=True),
                deps=[m_x1T, MW] + (t_pkvc.wdeps() if g == 0 else []))
        t_pkvc.wrote(mkv_mm)
        mkvb = ctx.s(lambda e, r=r: e.copy(kvbf[:, r], pkvc[:, 0:128]),
                     deps=[mkv_mm] + t_kvbf[r].wdeps())
        t_kvbf[r].wrote(mkvb)
        t_pkvc.read(mkvb)
        t_x1T[r].read(mkvb)
        mkvT_t = ctx.t(lambda e, r=r: e.transpose(pT[:, 512:640], kvbf[:, r],
                                                  idbf_s[:]),
                       deps=[mkvb] + t_pT.wdeps())
        t_pT.wrote(mkvT_t)
        mkvT = ctx.v(lambda e, i=i: e.tensor_copy(
            kvTf[:, i * 128:(i + 1) * 128], pT[:, 512:640]),
            deps=[mkvT_t] + t_kvT.wdeps())
        t_pT.read(mkvT)
        t_kvbf[r].read(mkvT)
        t_kvT.wrote(mkvT)
        yield
        mk_mm = ctx.t(lambda e, i=i: e.matmul(
            pkvc[:], kvTf[:, i * 128:(i + 1) * 128], wk_s[:], start=True, stop=True,
            skip_group_check=True),
            deps=[mkvT, MW] + t_pkvc.wdeps())
        t_pkvc.wrote(mk_mm)
        t_kvT.read(mk_mm)
        mSk = ctx.s(lambda e, r=r: e.activation(sq_big[:, r], pkvc[:], AF.Square),
                    deps=[mk_mm] + tk_sqh[r].wdeps())
        tk_sqh[r].wrote(mSk)
        t_pkvc.read(mSk)
        mredk = ctx.v(lambda e, r=r: e.reduce_sum(
            stats[:, 80 + r * 16:96 + r * 16],
            sq_big[:, r].rearrange("p (h j) -> p h j", j=32),
            axis=AX.X), deps=[mSk] + tk_rmsk[r].wdeps())
        tk_sqh[r].read(mredk)
        mlnk = ctx.s(lambda e, r=r: e.activation(
            stats[:, 80 + r * 16:96 + r * 16], stats[:, 80 + r * 16:96 + r * 16],
            AF.Ln, scale=1.0 / 32, bias=cb[:, 1:2]), deps=[mredk, MCB])
        mr8 = ctx.s(lambda e, i=i, r=r: e.activation(
            rms8[:, i, :], stats[:, 80 + r * 16:96 + r * 16], AF.Exp, scale=-0.5,
            bias=cb[:, 2:3]), deps=[mlnk, MCB])
        tk_rmsk[r].wrote(mr8)
        rms8m[i] = mr8
        t_rms8.wrote(mr8)
        yield
        mvc_mm = ctx.t(lambda e, i=i: e.matmul(
            pkvc[:], kvTf[:, i * 128:(i + 1) * 128], wv_s[:], start=True,
            stop=True, skip_group_check=True),
            deps=[mkvT, MW, mSk] + t_pkvc.wdeps())
        t_pkvc.wrote(mvc_mm)
        t_kvT.read(mvc_mm)
        m_vc = ctx.v(lambda e, i=i: e.tensor_copy(
            vc33[:, i, :].rearrange("p (h j) -> p h j", j=33)[:, :, 0:32],
            pkvc[:].rearrange("p (h j) -> p h j", j=32)), deps=[mvc_mm, m_vc1])
        t_pkvc.read(m_vc)
        t_vc33.wrote(m_vc)
        a0_done[i] = ("V", ctx.cnt["V"])
        yield

    a0_gen = [None]
    a0_cool = [0, 1]

    def a0_pump(force=False):
        if a0_gen[0] is None:
            return
        if not force and a0_cool[0] > 0:
            a0_cool[0] -= 1
            return
        try:
            next(a0_gen[0])
            a0_cool[0] = a0_cool[1]
        except StopIteration:
            a0_gen[0] = None

    def a0_start(ti):
        while a0_gen[0] is not None:
            a0_pump(force=True)
        a0_gen[0] = emit_a0_tile_gen(ti)
        next(a0_gen[0])
        a0_cool[0] = a0_cool[1] + 2

    def emit_a0_tile(i):
        for _ in emit_a0_tile_gen(i):
            pass
        return a0_done[i]

    t_psc = [Trk(), Trk(), Trk()]
    t_paux = Trk()
    t_qtl = [Trk(), Trk()]
    t_pav = Trk()
    t_expb = [Trk() for _ in range(6)]
    t_dens = Trk()
    t_recs = Trk()
    t_denrow = [Trk(), Trk()]
    t_ocn = Trk()
    t_Sall = Trk()
    t_x2s = [Trk(), Trk()]
    t_x3s = [Trk(), Trk()]
    t_lnt = [Trk(), Trk()]
    t_x3Th = [Trk(), Trk()]
    t_x3Tl = [Trk(), Trk()]
    m_x2st = [None] * NT
    a0_done = [None] * NT

    for i in range(4):
        a0_done[i] = emit_a0_tile(i)

    def emit_qtilde(c, h):
        hg, sl = h // 4, h % 4
        mqt = ctx.t(lambda e, h=h, sl=sl, hg=hg, c=c: e.matmul(
            paux[:], wkT_s[sl * 32:(sl + 1) * 32, h, :],
            qkT[sl * 32:(sl + 1) * 32, hg, c * 512:(c + 1) * 512],
            start=True, stop=True, tile_position=(sl * 32, 0),
            skip_group_check=True),
            deps=t_paux.wdeps() + [MW, a0_done[4 * c + 3]])
        t_paux.wrote(mqt)
        hb = h % 2
        dst = qtlA if hb == 0 else qtlB
        mqte = ctx.v(lambda e, dst=dst: e.tensor_copy(dst[:], paux[:]),
                     deps=[mqt] + t_qtl[hb].wdeps())
        t_paux.read(mqte)
        t_qtl[hb].wrote(mqte)
        return mqte

    for c in range(NC):
        n_st = 4 * c + 4
        m_qte = emit_qtilde(c, 0)
        for h in range(16):
            hg, sl = h // 4, h % 4
            hb = h % 2
            qsrc = qtlA if hb == 0 else qtlB
            mqte = m_qte
            mav = None
            mexs = [None] * (n_st + 4)
            EXPD = 6
            if c < NC - 1 and h % 4 == 0:
                a0_cool[1] = 2 if n_st >= 8 else 1
                a0_start(4 * (c + 1) + h // 4)
            for j in range(n_st):
                a0_pump()
                ps_ = j % 3
                vs = max(0, (j - 4 * c) * 128)
                msc = ctx.t(lambda e, ps_=ps_, j=j, vs=vs, qsrc=qsrc: e.matmul(
                    pscore[:, ps_, vs:512], kvTf[:, j * 128:(j + 1) * 128],
                    qsrc[:, vs:512],
                    start=True, stop=True, skip_group_check=True),
                    deps=t_psc[ps_].wdeps() + [mqte, a0_done[j]])
                t_psc[ps_].wrote(msc)
                jb = j % EXPD
                mex = ctx.s(lambda e, ps_=ps_, j=j, h=h, jb=jb, vs=vs: e.activation(
                    expT[:, jb, vs:512], pscore[:, ps_, vs:512], AF.Exp,
                    scale=rms8[:, j, h:h + 1]),
                    deps=t_psc[ps_].rdeps() + t_expb[jb].wdeps() + [rms8m[j]])
                t_psc[ps_].read(mex)
                t_expb[jb].wrote(mex)
                p_ = j - 4 * c
                if p_ >= 0:
                    mex = ctx.v(lambda e, jb=jb, vs=vs: e.tensor_tensor(
                        expT[:, jb, vs:vs + 128], expT[:, jb, vs:vs + 128],
                        cmask_s[:, 896:1024],
                        ALU.mult), deps=[mex, MCM])
                    t_expb[jb].wrote(mex)
                mexs[j] = mex
                if j == n_st - 2 and h < 15:
                    m_qte_next = emit_qtilde(c, h + 1)
                if j % 2 == 1 and j >= 3:
                    for jp in (j - 3, j - 2):
                        jpb = jp % EXPD
                        vjp = max(0, (jp - 4 * c) * 128)
                        mv = ctx.t(lambda e, h=h, jp=jp, jpb=jpb, vjp=vjp,
                                   first=(jp == 0): e.matmul(
                            pav[0:33, vjp:512], vc33[:, jp, h * 33:h * 33 + 33],
                            expT[:, jpb, vjp:512],
                            start=first, stop=False, skip_group_check=True),
                            deps=[mexs[jp]] + (t_pav.wdeps() if jp == 0 else []))
                        t_expb[jpb].read(mv)
                        mav = mv
                        if jp == 0:
                            t_pav.wrote(mv)
            last_odd = n_st - 1 if (n_st - 1) % 2 == 1 else n_st - 2
            for jp in range(last_odd - 1, n_st):
                jpb = jp % EXPD
                vjp = max(0, (jp - 4 * c) * 128)
                mv = ctx.t(lambda e, h=h, jp=jp, jpb=jpb, vjp=vjp,
                           first=(jp == 0), last=(jp == n_st - 1): e.matmul(
                    pav[0:33, vjp:512], vc33[:, jp, h * 33:h * 33 + 33],
                    expT[:, jpb, vjp:512],
                    start=first, stop=last, skip_group_check=True),
                    deps=[mexs[jp]] + (t_pav.wdeps() if jp == 0 else []))
                t_expb[jpb].read(mv)
                mav = mv
                if jp == 0:
                    t_pav.wrote(mv)
            t_pav.wrote(mav)
            t_qtl[hb].read(mexs[n_st - 1])
            if h < 15:
                m_qte = m_qte_next
            mcp = ctx.s(lambda e, sl=sl, hg=hg: e.copy(
                ocnT[sl * 32:(sl + 1) * 32, hg, :], pav[0:32, :]),
                deps=t_pav.rdeps() + t_ocn.wdeps())
            t_ocn.wrote(mcp)
            t_pav.read(mcp)
            par = h % 2
            mdr = ctx.v(lambda e, par=par: e.tensor_copy(
                denrow[0:1, par, :], pav[32:33, :]),
                deps=t_pav.rdeps() + t_denrow[par].wdeps())
            t_pav.read(mdr)
            t_denrow[par].wrote(mdr)
            mden = ctx.dma(lambda e, h=h, par=par: e.dma_start(
                out=dens[h:h + 1, :], in_=denrow[0:1, par, :]),
                deps=[mdr] + t_dens.wdeps(), q="Qs")
            t_denrow[par].read(mden)
            t_dens.wrote(mden)
        while a0_gen[0] is not None:
            a0_pump(force=True)
        mrcp = ctx.v(lambda e: e.reciprocal(recs[:], dens[:]),
                     deps=t_dens.wdeps() + t_recs.wdeps())
        t_dens.read(mrcp)
        t_recs.wrote(mrcp)
        for hg in range(4):
            mbc = ctx.t(lambda e, hg=hg: e.matmul(
                pav[:], hsel_s[:, hg, :], recs[:], start=True, stop=True,
                skip_group_check=True),
                deps=[mrcp, MHS] + t_pav.wdeps())
            t_pav.wrote(mbc)
            t_recs.read(mbc)
            mno = ctx.v(lambda e, hg=hg: e.tensor_tensor(
                ocnT[:, hg, :], ocnT[:, hg, :], pav[:], ALU.mult),
                deps=[mbc] + t_ocn.wdeps())
            t_pav.read(mno)
            t_ocn.wrote(mno)
        m_ocn_all = ("V", ctx.cnt["V"])
        ep_stageB = [None]
        for tt in range(4):
            it = c * 4 + tt
            r = it % 2
            mwo = None
            for n in range(2):
                for g in range(4):
                    mwo = ctx.t(lambda e, n=n, g=g, tt=tt: e.matmul(
                        pscore[:, n, :], ocnT[:, g, tt * 128:(tt + 1) * 128],
                        wo_s[:, g, n * 512:(n + 1) * 512],
                        start=(g == 0), stop=(g == 3), skip_group_check=True),
                        deps=[m_ocn_all, MW] + (t_psc[0].wdeps() + t_psc[1].wdeps()
                                                if (n == 0 and g == 0) else []))
            t_psc[0].wrote(mwo)
            t_psc[1].wrote(mwo)
            t_ocn.read(mwo)
            if ep_stageB[0] is not None:
                ep_stageB[0]()
                ep_stageB[0] = None
            mxre = ctx.dma(lambda e, it=it, r=r: e.dma_start(
                out=xin[:, r, :], in_=x[it * 128:(it + 1) * 128, :]),
                deps=t_xin[r].wdeps())
            t_xin[r].wrote(mxre)
            a_ap = pscore[:, 0:2, :].rearrange("p s n -> p (s n)")
            mo = layer_norm(a_ap, lnt[:, r, :], [mwo] + t_lnt[r].wdeps(), par=r)
            t_lnt[r].wrote(mo)
            t_psc[0].read(mo)
            t_psc[1].read(mo)
            t_ocn.read(mo)
            mx2 = ctx.v(lambda e, r=r: e.tensor_add(x2s[:, r, :], lnt[:, r, :],
                                                    xin[:, r, :]),
                        deps=[mo, mxre] + t_x2s[r].wdeps())
            t_lnt[r].read(mx2)
            t_xin[r].read(mx2)
            t_x2s[r].wrote(mx2)
            mst2 = ctx.dma(lambda e, it=it, r=r: e.dma_start(
                out=x2h[it * 128:(it + 1) * 128, :], in_=x2s[:, r, :]), deps=[mx2])
            t_x2s[r].read(mst2)
            m_x2st[it] = mst2
            if dbg:
                md = ctx.dma(lambda e, it=it, r=r: e.dma_start(
                    out=dbg_x2[it * 128:(it + 1) * 128, :], in_=x2s[:, r, :]),
                    deps=[mx2])
                t_x2s[r].read(md)
            mo3 = layer_norm(x2s[:, r, :], x3s[:, r, :], [mx2] + t_x3s[r].wdeps(),
                             par=r)
            t_x2s[r].read(mo3)
            t_x3s[r].wrote(mo3)

            def make_stageB(it=it, r=r, mo3=mo3):
                def sb():
                    mtr3a = None
                    for g in range(4):
                        mtr3a = ctx.t(lambda e, g=g, r=r: e.transpose(
                            pq[:, g * 128:(g + 1) * 128],
                            x3s[:, r, g * 128:(g + 1) * 128], idf_s[:]),
                            deps=[mo3, mc[1]] + (t_pq.wdeps() if g == 0 else []))
                    t_pq.wrote(mtr3a)
                    mtr3b = None
                    for g in range(4, 8):
                        mtr3b = ctx.t(lambda e, g=g, r=r: e.transpose(
                            pkvc[:, (g - 4) * 128:(g - 3) * 128],
                            x3s[:, r, g * 128:(g + 1) * 128], idf_s[:]),
                            deps=[mo3, mc[1]] + (t_pkvc.wdeps() if g == 4 else []))
                    t_pkvc.wrote(mtr3b)
                    t_x3s[r].read(mtr3a)
                    t_x3s[r].read(mtr3b)
                    mhiA = ctx.s(lambda e, r=r: e.copy(
                        x3Th[:, r, 0:4].rearrange("p g t -> p (g t)"), pq[:]),
                        deps=[mtr3a] + t_x3Th[r].wdeps())
                    mhiB = ctx.s(lambda e, r=r: e.copy(
                        x3Th[:, r, 4:8].rearrange("p g t -> p (g t)"), pkvc[:]),
                        deps=[mtr3b])
                    t_x3Th[r].wrote(mhiB)
                    t_pq.read(mhiA)
                    t_pkvc.read(mhiB)
                    mloA = ctx.v(lambda e, r=r: e.tensor_tensor(
                        x3Tl[:, r, 0:4].rearrange("p g t -> p (g t)"), pq[:],
                        x3Th[:, r, 0:4].rearrange("p g t -> p (g t)"), ALU.subtract),
                        deps=[mhiA] + t_x3Tl[r].wdeps())
                    mloB = ctx.v(lambda e, r=r: e.tensor_tensor(
                        x3Tl[:, r, 4:8].rearrange("p g t -> p (g t)"), pkvc[:],
                        x3Th[:, r, 4:8].rearrange("p g t -> p (g t)"), ALU.subtract),
                        deps=[mhiB])
                    t_x3Tl[r].wrote(mloB)
                    t_pq.read(mloA)
                    t_pkvc.read(mloB)
                    mhi = [mhiA, mhiB]
                    mlo = [mloA, mloB]
                    mhst = ctx.dma(lambda e, it=it, r=r: e.dma_start(
                        out=x3hiTh.rearrange("(g p) t -> p g t", g=8)[:, :, it * 128:(it + 1) * 128],
                        in_=x3Th[:, r]), deps=mhi)
                    t_x3Th[r].read(mhst)
                    mmS = ctx.t(lambda e, r=r: e.matmul(
                        paux[:, 0:48], x3Th[:, r, 0, :], nw_hi[:, 0, :],
                        start=True, stop=False, skip_group_check=True),
                        deps=mhi + mlo + MNW + t_paux.wdeps())
                    mmS = ctx.t(lambda e: e.matmul(paux[:, 0:48], ones1b_s[:], nb_hi[:],
                                                   start=False, stop=False,
                                                   skip_group_check=True), deps=[])
                    mmS = ctx.t(lambda e: e.matmul(paux[:, 0:48], ones1b_s[:], nb_lo[:],
                                                   start=False, stop=False,
                                                   skip_group_check=True), deps=[])
                    for g in range(8):
                        mmS = ctx.t(lambda e, g=g, r=r: e.matmul(
                            paux[:, 0:48], x3Th[:, r, g, :], nw_lo[:, g, :],
                            start=False, stop=False, skip_group_check=True), deps=[])
                    for g in range(8):
                        mmS = ctx.t(lambda e, g=g, r=r: e.matmul(
                            paux[:, 0:48], x3Tl[:, r, g, :], nw_hi[:, g, :],
                            start=False, stop=False, skip_group_check=True), deps=[])
                    for g in range(1, 8):
                        mmS = ctx.t(lambda e, g=g, r=r: e.matmul(
                            paux[:, 0:48], x3Th[:, r, g, :], nw_hi[:, g, :],
                            start=False, stop=(g == 7), skip_group_check=True), deps=[])
                    t_paux.wrote(mmS)
                    mse = ctx.v(lambda e, it=it: e.tensor_copy(
                        S_all[:, it, :], paux[:, 0:48]), deps=[mmS])
                    t_paux.read(mse)
                    t_x3Th[r].read(mse)
                    t_x3Tl[r].read(mse)
                    t_Sall.wrote(mse)
                return sb
            ep_stageB[0] = make_stageB()
        if ep_stageB[0] is not None:
            ep_stageB[0]()
            ep_stageB[0] = None

    mA1_Q = ("Qs", ctx.cnt["Qs"])
    stA.close()
    ctx.barrier()

    stB = ExitStack()
    sbB = lambda name, shape, dt: stB.enter_context(nc.sbuf_tensor(name, shape, dt))
    kTf = sbB("kTf", [128, 8, T], BF16)
    vS = sbB("vS", [128, NT, 1026], BF16)
    stSEL = ExitStack()
    selT = stSEL.enter_context(nc.sbuf_tensor("selT", [16, 3, T], BF16))

    stR = ExitStack()
    sbR = lambda name, shape, dt: stR.enter_context(nc.sbuf_tensor(name, shape, dt))
    psR = lambda name, shape, dt: stR.enter_context(nc.psum_tensor(name, shape, dt))
    selt = sbR("selt", [128, NT, 48], BF16)
    routT = sbR("routT", [128, NT, 34], F32)
    pTr = psR("pTr", [16, 128], BF16)
    mS_done = ("V", ctx.cnt["V"])
    if dbg:
        dbS_s = sbR("dbS_s", [128, NT * 48], F32)
        mdbS = ctx.v(lambda e: e.tensor_copy(
            dbS_s[:], S_all[:].rearrange("p t n -> p (t n)")), deps=[mS_done])
        ctx.dma(lambda e: e.dma_start(out=dbg_S[:, :], in_=dbS_s[:]), deps=[mdbS])

    m_sel_done = []
    for fi in range(3):
        base = fi * 16
        mH = ctx.v(lambda e, base=base: e.tensor_single_scalar(
            routT[:, :, 0:15], S_all[:, :, base:base + 15], 0.0, ALU.is_ge),
            deps=[mS_done] + (m_sel_done[-1:] if m_sel_done else []))
        mHn = ctx.v(lambda e: e.tensor_scalar(
            routT[:, :, 15:30], routT[:, :, 0:15], -1.0, 1.0, ALU.mult, ALU.add),
            deps=[mH])
        d2 = [mHn]
        for n4 in range(4):
            a_, b_ = divmod(n4, 2)
            pcol = 15 if a_ == 0 else 0
            ccol = (15 + 1 + a_) if b_ == 0 else (1 + a_)
            m2 = ctx.v(lambda e, n4=n4, pcol=pcol, ccol=ccol: e.tensor_tensor(
                routT[:, :, 30 + n4], routT[:, :, pcol], routT[:, :, ccol], ALU.mult),
                deps=d2)
            d2 = [m2]
        for n8 in range(8):
            a_, b_ = divmod(n8, 2)
            node = 3 + a_
            ccol = (15 + node) if b_ == 0 else node
            m3 = ctx.v(lambda e, n8=n8, a_=a_, ccol=ccol, base=base: e.tensor_tensor(
                S_all[:, :, base + n8], routT[:, :, 30 + a_], routT[:, :, ccol],
                ALU.mult), deps=d2)
            d2 = [m3]
        for l in range(16):
            a_, b_ = divmod(l, 2)
            node = 7 + a_
            ccol = (15 + node) if b_ == 0 else node
            msl = ctx.v(lambda e, l=l, a_=a_, ccol=ccol, base=base: e.tensor_tensor(
                selt[:, :, base + l], S_all[:, :, base + a_], routT[:, :, ccol],
                ALU.mult), deps=d2)
            d2 = [msl]
        m_sel_done.append(d2[0])
    if dbg:
        dbsl = sbR("dbsl", [128, NT * 48], F32)
        mdb = ctx.v(lambda e: e.tensor_copy(
            dbsl[:], selt[:].rearrange("p t n -> p (t n)")), deps=m_sel_done)
        ctx.dma(lambda e: e.dma_start(out=dbg_sel[:, :], in_=dbsl[:]), deps=[mdb])
    t_pTr = Trk()
    m_selT = None
    for fi in range(3):
        for i in range(NT):
            mt_ = ctx.t(lambda e, fi=fi, i=i: e.transpose(
                pTr[:], selt[:, i, fi * 16:(fi + 1) * 16], idbf_s[:]),
                deps=[m_sel_done[fi]] + t_pTr.wdeps())
            t_pTr.wrote(mt_)
            m_selT = ctx.v(lambda e, fi=fi, i=i: e.tensor_copy(
                selT[:, fi, i * 128:(i + 1) * 128], pTr[:]), deps=[mt_])
            t_pTr.read(m_selT)
    mselT = ("V", ctx.cnt["V"])

    stR.close()
    ctx.barrier()

    stW1 = ExitStack()
    sbW1 = lambda name, shape, dt: stW1.enter_context(nc.sbuf_tensor(name, shape, dt))
    psW1 = lambda name, shape, dt: stW1.enter_context(nc.psum_tensor(name, shape, dt))
    w1_s = sbW1("w1_s", [128, 8, 1024], BF16)
    b1_s = sbW1("b1_s", [128, 8], F32)
    hm_s = sbW1("hm_s", [128, 8, 512], BF16)
    relu_s = sbW1("relu_s", [128, 1024], F32)
    x3c_r = sbW1("x3c_r", [128, 2, 8, 512], BF16)
    pH = psW1("pH", [128, 1024], F32)
    pM = psW1("pM", [128, 1024], F32)

    t_w1 = Trk()
    t_b1 = Trk()
    t_hm = [Trk() for _ in range(8)]
    t_relu = [Trk(), Trk()]
    t_pH = [Trk(), Trk()]
    t_pM = [Trk(), Trk()]
    t_x3c = [Trk(), Trk(), Trk()]
    m_hf_st = {f: [None] * NC for f in "qkv"}
    for fi, f in enumerate(["q", "k", "v"]):
        mw = None
        for g in range(8):
            mw = ctx.dma(lambda e, g=g, f=f: e.dma_start(
                out=w1_s[:, g, :], in_=Wff[f]["w1"][g * 128:(g + 1) * 128, :]), q="Qg",
                deps=(t_w1.wdeps() if g == 0 else []))
        mwb = ctx.dma(lambda e, f=f: e.dma_start(out=b1_s[:], in_=Wff[f]["b1"][:, :]),
                      q="Qg", deps=t_b1.wdeps())
        MWF = ("Qg", ctx.cnt["Qg"])
        t_w1.wrote(MWF)
        t_b1.wrote(MWF)
        for c in range(NC):
            rc_ = c % 2
            mx3 = ctx.dma(lambda e, c=c, rc_=rc_: e.dma_start(
                out=x3c_r[:, rc_],
                in_=x3hiTh.rearrange("(g p) t -> p g t", g=8)[:, :, c * 512:(c + 1) * 512]),
                deps=t_x3c[rc_].wdeps() + [mA1_Q], q=("Qs" if c % 2 == 0 else "Qg"))
            t_x3c[rc_].wrote(mx3)
            for m in range(8):
                pp = m % 2
                mH1 = None
                for g in range(8):
                    mH1 = ctx.t(lambda e, m=m, g=g, rc_=rc_, pp=pp: e.matmul(
                        pH[:, pp * 512:(pp + 1) * 512],
                        w1_s[:, g, m * 128:(m + 1) * 128],
                        x3c_r[:, rc_, g, :],
                        start=(g == 0), stop=(g == 7), skip_group_check=True),
                        deps=[MWF, mx3] + (t_pH[pp].wdeps() if g == 0 else []))
                t_pH[pp].wrote(mH1)
                mM1 = ctx.t(lambda e, m=m, fi=fi, c=c, pp=pp: e.matmul(
                    pM[:, pp * 512:(pp + 1) * 512],
                    esel_s[:, m * 128:(m + 1) * 128],
                    selT[:, fi, c * 512:(c + 1) * 512],
                    start=True, stop=True, skip_group_check=True),
                    deps=[mselT, MC] + t_pM[pp].wdeps())
                t_pM[pp].wrote(mM1)
                mre = ctx.s(lambda e, m=m, pp=pp: e.activation(
                    relu_s[:, pp * 512:(pp + 1) * 512], pH[:, pp * 512:(pp + 1) * 512],
                    AF.Relu, bias=b1_s[:, m:m + 1]),
                    deps=[mH1, MWF] + t_relu[pp].wdeps())
                t_relu[pp].wrote(mre)
                t_pH[pp].read(mre)
                t_w1.read(mre)
                t_b1.read(mre)
                t_x3c[rc_].read(mre)
                mhm = ctx.v(lambda e, m=m, pp=pp: e.tensor_tensor(
                    hm_s[:, m, :], relu_s[:, pp * 512:(pp + 1) * 512],
                    pM[:, pp * 512:(pp + 1) * 512], ALU.mult),
                    deps=[mre, mM1] + t_hm[m].wdeps())
                t_relu[pp].read(mhm)
                t_pM[pp].read(mhm)
                t_hm[m].wrote(mhm)
                mhv = ctx.dma(lambda e, m=m, c=c, f=f: e.dma_start(
                    out=hfh[f].rearrange("(m p) t -> p m t", m=8)[:, m, c * 512:(c + 1) * 512],
                    in_=hm_s[:, m, :]), deps=[mhm], q="Qg")
                t_hm[m].read(mhv)
                m_hf_st[f][c] = mhv
    stW1.close()
    ctx.barrier()

    stW2 = ExitStack()
    sbW2 = lambda name, shape, dt: stW2.enter_context(nc.sbuf_tensor(name, shape, dt))
    psW2 = lambda name, shape, dt: stW2.enter_context(nc.psum_tensor(name, shape, dt))
    w2_s = sbW2("w2_s", [128, 8, 1024], BF16)
    b2_s = sbW2("b2_s", [16, 1024], BF16)
    hm_r = sbW2("hm_r", [128, 2, 8, 512], BF16)
    qst = sbW2("qst", [128, 2, 512], BF16)
    pO = psW2("pO", [128, 2048], F32)

    t_w2 = Trk()
    t_b2 = Trk()
    t_hmr = [Trk(), Trk()]
    t_pO = [Trk(), Trk()]
    t_qst = [Trk(), Trk()]
    t_vS = Trk()
    t_kTf = Trk()
    m_v1s = ctx.v(lambda e: e.memset(vS[:, :, 1024], 1.0))
    for fi, f in enumerate(["q", "k", "v"]):
        mw = None
        for g in range(8):
            mw = ctx.dma(lambda e, g=g, f=f: e.dma_start(
                out=w2_s[:, g, :], in_=Wff[f]["w2"][g * 128:(g + 1) * 128, :]), q="Qg",
                deps=(t_w2.wdeps() if g == 0 else []))
        mwb2 = ctx.dma(lambda e, f=f: e.dma_start(out=b2_s[:], in_=Wff[f]["b2"][:, :]),
                       q="Qg", deps=t_b2.wdeps())
        MWF2 = ("Qg", ctx.cnt["Qg"])
        t_b2.wrote(MWF2)
        t_w2.wrote(MWF2)
        for c in range(NC):
            rc_ = c % 2
            mhl = ctx.dma(lambda e, c=c, rc_=rc_, f=f: e.dma_start(
                out=hm_r[:, rc_].rearrange("p m t -> p (m t)"),
                in_=hfh[f].rearrange("(m p) t -> p m t", m=8)[:, :, c * 512:(c + 1) * 512]),
                deps=t_hmr[rc_].wdeps() + [m_hf_st[f][c]],
                q=("Qs" if c % 2 == 0 else "Qg"))
            t_hmr[rc_].wrote(mhl)
            if f in ("q", "k"):
                for o in range(8):
                    po = o % 2
                    mo_ = None
                    for m in range(8):
                        mo_ = ctx.t(lambda e, o=o, m=m, rc_=rc_, po=po: e.matmul(
                            pO[:, po * 512:(po + 1) * 512],
                            w2_s[:, m, o * 128:(o + 1) * 128],
                            hm_r[:, rc_, m, :], start=(m == 0), stop=False,
                            skip_group_check=True),
                            deps=[mhl, MWF2] + (t_pO[po].wdeps() if m == 0 else []))
                    mo_ = ctx.t(lambda e, o=o, fi=fi, c=c, po=po: e.matmul(
                        pO[:, po * 512:(po + 1) * 512],
                        b2_s[:, o * 128:(o + 1) * 128],
                        selT[:, fi, c * 512:(c + 1) * 512],
                        start=False, stop=True, skip_group_check=True),
                        deps=[mwb2, mselT])
                    t_pO[po].wrote(mo_)
                    if f == "k":
                        mev = ctx.v(lambda e, o=o, c=c, po=po: e.tensor_copy(
                            kTf[:, o, c * 512:(c + 1) * 512], pO[:, po * 512:(po + 1) * 512]),
                            deps=[mo_])
                        t_kTf.wrote(mev)
                        t_hmr[rc_].read(mev)
                        t_w2.read(mev)
                        t_b2.read(mev)
                    else:
                        rq = (c * 8 + o) % 2
                        mev = ctx.v(lambda e, rq=rq, po=po: e.tensor_copy(
                            qst[:, rq, :], pO[:, po * 512:(po + 1) * 512]),
                            deps=[mo_] + t_qst[rq].wdeps())
                        t_qst[rq].wrote(mev)
                        t_hmr[rc_].read(mev)
                        t_w2.read(mev)
                        t_b2.read(mev)
                        mq_st = ctx.dma(lambda e, o=o, c=c, rq=rq: e.dma_start(
                            out=qTh[o * 128:(o + 1) * 128, c * 512:(c + 1) * 512],
                            in_=qst[:, rq, :]), deps=[mev], q="Qg")
                        t_qst[rq].read(mq_st)
                        if dbg:
                            mq2 = ctx.dma(lambda e, o=o, c=c, rq=rq: e.dma_start(
                                out=dbg_qT[o * 128:(o + 1) * 128, c * 512:(c + 1) * 512],
                                in_=qst[:, rq, :]), deps=[mev])
                            t_qst[rq].read(mq2)
                    t_pO[po].read(mev)
            else:
                for tt in range(4):
                    it = c * 4 + tt
                    pv = tt % 2
                    mv_ = None
                    for oc in range(2):
                        for m in range(8):
                            mv_ = ctx.t(lambda e, m=m, oc=oc, tt=tt, rc_=rc_, pv=pv: e.matmul(
                                pO[:, pv * 1024 + oc * 512:pv * 1024 + (oc + 1) * 512],
                                hm_r[:, rc_, m, tt * 128:(tt + 1) * 128],
                                w2_s[:, m, oc * 512:(oc + 1) * 512],
                                start=(m == 0), stop=False, skip_group_check=True),
                                deps=[mhl, MWF2] + (t_pO[pv].wdeps() if (oc == 0 and m == 0) else []))
                        mv_ = ctx.t(lambda e, oc=oc, it=it, pv=pv: e.matmul(
                            pO[:, pv * 1024 + oc * 512:pv * 1024 + (oc + 1) * 512],
                            selT[:, 2, it * 128:(it + 1) * 128],
                            b2_s[:, oc * 512:(oc + 1) * 512],
                            start=False, stop=True, skip_group_check=True),
                            deps=[mselT])
                    t_pO[pv].wrote(mv_)
                    mvev = ctx.v(lambda e, it=it, pv=pv: e.tensor_copy(
                        vS[:, it, 0:1024], pO[:, pv * 1024:(pv + 1) * 1024]),
                        deps=[mv_, m_v1s])
                    t_pO[pv].read(mvev)
                    t_hmr[rc_].read(mvev)
                    t_w2.read(mvev)
                    t_b2.read(mvev)
                    t_vS.wrote(mvev)
    MQST = ("Qg", ctx.cnt["Qg"])
    MVS = ("V", ctx.cnt["V"])
    if dbg:
        for i in range(NT):
            ctx.dma(lambda e, i=i: e.dma_start(
                out=dbg_v[:, i * 1024:(i + 1) * 1024], in_=vS[:, i, 0:1024]),
                deps=[MVS])
        for g in range(8):
            ctx.dma(lambda e, g=g: e.dma_start(
                out=dbg_kTb[g * 128:(g + 1) * 128, :], in_=kTf[:, g, :]),
                deps=[t_kTf.w, MVS])
    stW2.close()
    stSEL.close()
    ctx.barrier()

    stF = ExitStack()
    sbF = lambda name, shape, dt: stF.enter_context(nc.sbuf_tensor(name, shape, dt))
    psF = lambda name, shape, dt: stF.enter_context(nc.psum_tensor(name, shape, dt))
    qTc = sbF("qTc", [128, 2, 8, 512], BF16)
    expc = sbF("expc", [128, NT, 512], BF16)
    f_s = sbF("f_s", [128, 1024], F32)
    outs = sbF("outs", [128, 2, 1024], F32)
    x2r = sbF("x2r", [128, 2, 1024], F32)
    recf = sbF("recf", [128, 1], F32)
    pSc = psF("pSc", [128, 1024], F32)
    pF = psF("pF", [128, 2048], F32)
    pDen = psF("pDen", [128, 16], F32)
    t_qTc = [Trk(), Trk()]
    t_expc = [Trk() for _ in range(NT)]
    t_pScb = [Trk(), Trk()]
    t_pF = [Trk(), Trk()]
    t_pDen = [Trk(), Trk()]
    t_fs = Trk()
    t_outs = [Trk(), Trk()]
    t_x2r = [Trk(), Trk()]
    t_recf = Trk()

    m_final = None
    for c in range(NC):
        r = c % 2
        mql = ctx.dma(lambda e, c=c, r=r: e.dma_start(
            out=qTc[:, r].rearrange("p g t -> p (g t)"),
            in_=qTh.rearrange("(g p) t -> p g t", g=8)[:, :, c * 512:(c + 1) * 512]),
            deps=t_qTc[r].wdeps() + [MQST])
        t_qTc[r].wrote(mql)
        for j in range(NT):
            jr = j % 2
            msc = None
            for g in range(8):
                msc = ctx.t(lambda e, g=g, j=j, r=r, jr=jr: e.matmul(
                    pSc[:, jr * 512:(jr + 1) * 512], kTf[:, g, j * 128:(j + 1) * 128],
                    qTc[:, r, g, :], start=(g == 0), stop=(g == 7),
                    skip_group_check=True),
                    deps=[mql, t_kTf.w, MVS] + (t_pScb[jr].wdeps() if g == 0 else []))
            t_pScb[jr].wrote(msc)
            mex = ctx.s(lambda e, j=j, jr=jr: e.activation(
                expc[:, j, :], pSc[:, jr * 512:(jr + 1) * 512], AF.Exp,
                scale=1.0 / 32.0), deps=[msc] + t_expc[j].wdeps())
            t_pScb[jr].read(mex)
            t_qTc[r].read(mex)
            t_expc[j].wrote(mex)
        for tt in range(4):
            it = c * 4 + tt
            rr = it % 2
            pf = tt % 2
            mf_ = None
            md_ = None
            for j in range(NT):
                mf_ = ctx.t(lambda e, j=j, tt=tt, pf=pf: e.matmul(
                    pF[:, pf * 1024:pf * 1024 + 512], expc[:, j, tt * 128:(tt + 1) * 128],
                    vS[:, j, 0:512], start=(j == 0), stop=(j == NT - 1),
                    skip_group_check=True),
                    deps=[t_expc[j].w, MVS] + (t_pF[pf].wdeps() if j == 0 else []))
                mf_ = ctx.t(lambda e, j=j, tt=tt, pf=pf: e.matmul(
                    pF[:, pf * 1024 + 512:pf * 1024 + 1024],
                    expc[:, j, tt * 128:(tt + 1) * 128],
                    vS[:, j, 512:1024], start=(j == 0), stop=(j == NT - 1),
                    skip_group_check=True), deps=[])
                md_ = ctx.t(lambda e, j=j, tt=tt, pf=pf: e.matmul(
                    pDen[:, pf * 8:pf * 8 + 1], expc[:, j, tt * 128:(tt + 1) * 128],
                    vS[:, j, 1024:1025], start=(j == 0), stop=(j == NT - 1),
                    skip_group_check=True),
                    deps=(t_pDen[pf].wdeps() if j == 0 else []))

            t_pF[pf].wrote(mf_)
            t_pDen[pf].wrote(md_)
            mrc = ctx.v(lambda e, pf=pf: e.reciprocal(recf[:], pDen[:, pf * 8:pf * 8 + 1]),
                        deps=[md_] + t_recf.wdeps())
            t_pDen[pf].read(mrc)
            t_recf.wrote(mrc)
            mfs = ctx.v(lambda e, pf=pf: e.tensor_scalar(
                f_s[:], pF[:, pf * 1024:(pf + 1) * 1024], recf[:], None, ALU.mult),
                deps=[mf_, mrc] + t_fs.wdeps())
            t_pF[pf].read(mfs)
            t_recf.read(mfs)
            t_fs.wrote(mfs)
            if tt == 3:
                for j in range(NT):
                    t_expc[j].read(mfs)
            mx2l = ctx.dma(lambda e, it=it, rr=rr: e.dma_start(
                out=x2r[:, rr, :], in_=x2h[it * 128:(it + 1) * 128, :]),
                deps=t_x2r[rr].wdeps() + [m_x2st[it]])
            t_x2r[rr].wrote(mx2l)
            mo_ = layer_norm(f_s[:], outs[:, rr, :], [mfs] + t_outs[rr].wdeps(), par=rr)
            t_fs.read(mo_)
            mof = ctx.v(lambda e, rr=rr: e.tensor_add(outs[:, rr, :], outs[:, rr, :],
                                                      x2r[:, rr, :]),
                        deps=[mo_, mx2l])
            t_x2r[rr].read(mof)
            t_outs[rr].wrote(mof)
            m_final = ctx.dma(lambda e, it=it, rr=rr: e.dma_start(
                out=out[it * 128:(it + 1) * 128, :], in_=outs[:, rr, :]), deps=[mof],
                q="Qg")
            t_outs[rr].read(m_final)

    ctx.finalize([("Qs", ctx.cnt["Qs"]), ("Qg", ctx.cnt["Qg"])])
    stF.close()
    stB.close()
    st.close()
    return nc


def make_inputs(x_shard, W):
    import ml_dtypes
    T = x_shard.shape[0]
    wo_rows = np.concatenate([W["L0_attn_Wo"][h * 64:h * 64 + 32, :] for h in range(16)],
                             axis=0)
    nwT = np.zeros((1024, 48), dtype=np.float32)
    nb = np.zeros((1, 48), dtype=np.float32)
    for fi, f in enumerate("qkv"):
        nwT[:, fi * 16:fi * 16 + 15] = np.asarray(W[f"L0_ffn_{f}_node_w"]).T
        nb[0, fi * 16:fi * 16 + 15] = np.asarray(W[f"L0_ffn_{f}_node_b"])
    E = np.zeros((16, 1024), dtype=np.float32)
    for m in range(8):
        E[2 * m, m * 128:m * 128 + 64] = 1.0
        E[2 * m + 1, m * 128 + 64:m * 128 + 128] = 1.0
    M = (np.arange(1920)[None, :] - 896 >= np.arange(128)[:, None]).astype(np.float32)
    hs = np.zeros((16, 4, 128), dtype=np.float32)
    for hg in range(4):
        for p in range(128):
            hs[4 * hg + p // 32, hg, p] = 1.0
    WkT = np.asarray(W["L0_attn_Wk"]).reshape(128, 16, 32).transpose(2, 1, 0)
    WkT = np.ascontiguousarray(np.tile(WkT, (4, 1, 1)))
    d = {
        "x": np.ascontiguousarray(x_shard),
        "WkT": WkT,
        "Wq": W["L0_attn_Wq"], "Wkv": W["L0_attn_Wkv"], "Wk": W["L0_attn_Wk"],
        "Wv": W["L0_attn_Wv"], "WoS": np.ascontiguousarray(wo_rows),
        "qg": np.ascontiguousarray(np.tile(np.asarray(W["L0_attn_q_g"])[None, :], (128, 1))),
        "kg": np.ascontiguousarray(np.tile(np.asarray(W["L0_attn_k_g"])[None, :], (128, 1))),
        "nwT": np.ascontiguousarray(nwT), "nbrow": nb,
        "idbf": np.eye(128, dtype=np.float32).astype(ml_dtypes.bfloat16),
        "idf": np.eye(128, dtype=np.float32),
        "ones1f": np.ones((1, 128), dtype=np.float32),
        "esel": E.astype(ml_dtypes.bfloat16),
        "cmask": M.astype(ml_dtypes.bfloat16),
        "hsel": np.ascontiguousarray(hs.reshape(16, 512)),
    }
    for f in "qkv":
        d[f"W1{f}"] = np.ascontiguousarray(
            np.asarray(W[f"L0_ffn_{f}_w1"]).transpose(1, 0, 2).reshape(1024, 1024))
        d[f"W2{f}"] = np.ascontiguousarray(np.asarray(W[f"L0_ffn_{f}_w2"]).reshape(1024, 1024))
        d[f"b1{f}"] = np.ascontiguousarray(
            np.asarray(W[f"L0_ffn_{f}_b1"]).reshape(1024).reshape(8, 128).T)
        d[f"b2{f}"] = np.ascontiguousarray(np.asarray(W[f"L0_ffn_{f}_b2"]))
    return d


_NC_CACHE = {}


def kernel(**inputs):
    T = 4096
    if "prog" not in _NC_CACHE:
        _NC_CACHE["prog"] = build(T, dbg=False)
    nc = _NC_CACHE["prog"]
    x = np.asarray(inputs["L0_attn_x"], dtype=np.float32)
    B = x.shape[0]
    base = make_inputs(np.ascontiguousarray(x[0]), inputs)
    in_maps = []
    for b in range(B):
        m = dict(base)
        m["x"] = np.ascontiguousarray(x[b])
        in_maps.append(m)
    from concourse.bass_utils import run_bass_kernel_spmd
    res = run_bass_kernel_spmd(nc, in_maps, core_ids=list(range(B)))
    return np.stack([r["out"] for r in res.results], axis=0)


# version 21
# speedup vs baseline: 1.0415x; 1.0415x over previous
import math
from contextlib import ExitStack

import numpy as np

import concourse.bass as bass
import concourse.mybir as mybir

F32 = mybir.dt.float32
BF16 = mybir.dt.bfloat16
AF = mybir.ActivationFunctionType
ALU = mybir.AluOpType
AX = mybir.AxisListType

D = 1024
EPS_LN = 1e-5
EPS_RMS = 1e-6
NEG_LN8 = -math.log(8.0)


class Trk:

    def __init__(self):
        self.w = None
        self.rs = []

    def wdeps(self):
        return [self.w] + self.rs

    def wrote(self, mark):
        self.w = mark
        self.rs = []

    def rdeps(self):
        return [self.w]

    def read(self, mark):
        self.rs.append(mark)


class Ctx:
    ENG = ("tensor", "vector", "scalar", "sync", "gpsimd")
    SEM_OF_ENG = {"T": "tensor"}

    def __init__(self, nc, stack):
        self.nc = nc
        self.streams = {e: [] for e in self.ENG}
        self.sems = {}
        self.cnt = {}
        self.last_wait = {e: {} for e in self.ENG}
        self.pending = {e: [] for e in self.ENG}
        for s in ("T", "V", "S", "Qs", "Qg"):
            self.sems[s] = stack.enter_context(nc.semaphore(s))
            self.cnt[s] = 0

    def op(self, eng, fn, deps=(), inc=None, amt=1):
        deps = list(deps) + self.pending[eng]
        self.pending[eng] = []
        dd = {}
        for d in deps:
            if d is None or self.SEM_OF_ENG.get(d[0]) == eng:
                continue
            dd[d[0]] = max(dd.get(d[0], 0), d[1])
        deps = tuple(sorted(dd.items()))
        mark = None
        if inc is not None:
            self.cnt[inc] += amt
            mark = (inc, self.cnt[inc])
        lw = self.last_wait[eng]
        sems = self.sems

        def emit(e, _fn=fn, _deps=deps, _inc=inc, _amt=amt):
            for s, c in _deps:
                if lw.get(s, 0) < c:
                    e.wait_ge(sems[s], c)
                    lw[s] = c
            inst = _fn(e)
            if _inc is not None and inst is not None:
                inst.then_inc(sems[_inc], _amt)

        self.streams[eng].append(emit)
        return mark

    def t(self, fn, deps=()):
        return self.op("tensor", fn, deps, "T")

    def v(self, fn, deps=()):
        return self.op("vector", fn, deps, "V")

    def s(self, fn, deps=()):
        return self.op("scalar", fn, deps, "S")

    def dma(self, fn, deps=(), q="Qs"):
        eng = "sync" if q == "Qs" else "gpsimd"
        return self.op(eng, fn, deps, q, 16)

    def barrier(self):
        marks = [(s, self.cnt[s]) for s in ("T", "V", "S", "Qs", "Qg")]
        for e in self.ENG:
            self.pending[e] = list(marks)

    def finalize(self, final_deps):
        self.op("sync", lambda e: None, deps=final_deps)
        nc = self.nc
        with nc.Block() as block:
            @block.tensor
            def _(t):
                for f in self.streams["tensor"]:
                    f(t)

            @block.vector
            def _(v):
                for f in self.streams["vector"]:
                    f(v)

            @block.scalar
            def _(s):
                for f in self.streams["scalar"]:
                    f(s)

            @block.sync
            def _(s):
                for f in self.streams["sync"]:
                    f(s)

            @block.gpsimd
            def _(g):
                for f in self.streams["gpsimd"]:
                    f(g)


def build(T, dbg=False, route_terms=(1,1,1,1)):
    NT = T // 128
    NC = T // 512
    nc = bass.Bass("TRN2", target_bir_lowering=False, debug=False)
    st = ExitStack()
    ctx = Ctx(nc, st)

    dr = lambda name, shape, dt, kind: nc.dram_tensor(name, shape, dt, kind=kind).ap()
    I, O, N_ = "ExternalInput", "ExternalOutput", "Internal"

    x = dr("x", [T, D], F32, I)
    Wq = dr("Wq", [D, D], F32, I)
    Wkv = dr("Wkv", [D, 128], F32, I)
    Wk = dr("Wk", [128, 512], F32, I)
    WkTd = dr("WkT", [128, 16 * 128], F32, I)
    Wv = dr("Wv", [128, 512], F32, I)
    WoS = dr("WoS", [512, D], F32, I)
    qg = dr("qg", [128, 64], F32, I)
    kg = dr("kg", [128, 32], F32, I)
    nwT = dr("nwT", [D, 48], F32, I)
    nbrow = dr("nbrow", [1, 48], F32, I)
    Wff = {f: dict(
        w1=dr(f"W1{f}", [D, D], F32, I), w2=dr(f"W2{f}", [D, D], F32, I),
        b1=dr(f"b1{f}", [128, 8], F32, I), b2=dr(f"b2{f}", [16, D], F32, I))
        for f in "qkv"}
    idbf = dr("idbf", [128, 128], BF16, I)
    idf = dr("idf", [128, 128], F32, I)
    ones1f = dr("ones1f", [1, 128], F32, I)
    esel = dr("esel", [16, 1024], BF16, I)
    cmask = dr("cmask", [128, 1920], BF16, I)
    hsel = dr("hsel", [16, 4 * 128], BF16, I)

    out = dr("out", [T, D], F32, O)
    x2h = dr("x2h", [T, D], F32, N_)
    x3hiTh = dr("x3hiTh", [D, T], BF16, N_)
    qTh = dr("qTh", [D, T], BF16, N_)
    hfh = {f: dr(f"h{f}h", [D, T], BF16, N_) for f in "qkv"}
    if dbg:
        dbg_x1 = dr("dbg_x1", [T, D], BF16, O)
        dbg_qkT = dr("dbg_qkT", [128, 4 * T], BF16, O)
        dbg_vc = dr("dbg_vc", [128, 528 * (T // 128)], BF16, O)
        dbg_rms8 = dr("dbg_rms8", [128, 16 * (T // 128)], F32, O)
        dbg_x2 = dr("dbg_x2", [T, D], F32, O)
        dbg_ocn = dr("dbg_ocn", [128, 4 * 1024], BF16, O)
        dbg_qtl = dr("dbg_qtl", [128, 2 * 1024], BF16, O)
        dbg_S = dr("dbg_S", [128, NT * 48], F32, O)
        dbg_hiT = dr("dbg_hiT", [D, T], BF16, O)
        dbg_sel = dr("dbg_sel", [128, NT * 48], F32, O)
        dbg_qT = dr("dbg_qT", [D, T], BF16, O)
        dbg_kTb = dr("dbg_kTb", [D, T], BF16, O)
        dbg_v = dr("dbg_v", [128, NT * 1024], BF16, O)

    sb = lambda name, shape, dt: st.enter_context(nc.sbuf_tensor(name, shape, dt))

    idbf_s = sb("idbf_s", [128, 128], BF16)
    idf_s = sb("idf_s", [128, 128], F32)
    ones1f_s = sb("ones1f_s", [1, 128], F32)
    ones1b_s = sb("ones1b_s", [1, 128], BF16)
    esel_s = sb("esel_s", [16, 1024], BF16)
    stats = sb("stats", [128, 128], F32)
    bnst = sb("bnst", [128, 24], F32)
    cb = sb("cb", [128, 4], F32)
    S_all = sb("S_all", [128, 32, 48], F32)
    mc = []
    mc.append(ctx.dma(lambda e: e.dma_start(out=idbf_s[:], in_=idbf[:, :]), q="Qg"))
    mc.append(ctx.dma(lambda e: e.dma_start(out=idf_s[:], in_=idf[:, :]), q="Qg"))
    mc.append(ctx.dma(lambda e: e.dma_start(out=ones1f_s[:], in_=ones1f[:, :]), q="Qg"))
    mc.append(ctx.dma(lambda e: e.dma_start(out=esel_s[:], in_=esel[:, :]), q="Qg"))
    m_ones_b = ctx.v(lambda e: e.tensor_copy(ones1b_s[:], ones1f_s[:]), deps=[mc[2]])
    ctx.v(lambda e: e.memset(cb[:, 0:1], EPS_LN))
    ctx.v(lambda e: e.memset(cb[:, 1:2], EPS_RMS))
    MCB = ctx.v(lambda e: e.memset(cb[:, 2:3], NEG_LN8))
    MC = ("Qg", ctx.cnt["Qg"])

    tk_sq = Trk()
    tk_ln = [Trk(), Trk()]
    tk_stq = Trk()
    tk_stk = Trk()

    def layer_norm(xap, out_ap, deps_in, par=0, n=D):
        bb = par * 12
        sb_ = 0 if par == 0 else 8
        deps_in = list(deps_in)
        b1 = ctx.v(lambda e: e.bn_stats(bnst[:, bb:bb + 6], xap[:, 0:512]),
                   deps=deps_in + tk_ln[par].wdeps())
        b2 = ctx.v(lambda e: e.bn_stats(bnst[:, bb + 6:bb + 12], xap[:, 512:1024]),
                   deps=deps_in)
        ag = ctx.v(lambda e: e.bn_aggr(stats[:, sb_:sb_ + 2], bnst[:, bb:bb + 12]),
                   deps=[b1, b2])
        ml = ctx.s(lambda e: e.activation(stats[:, sb_ + 2:sb_ + 3], stats[:, sb_ + 1:sb_ + 2],
                                          AF.Ln, bias=cb[:, 0:1]), deps=[ag, MCB])
        mr = ctx.s(lambda e: e.activation(stats[:, sb_ + 3:sb_ + 4], stats[:, sb_ + 2:sb_ + 3],
                                          AF.Exp, scale=-0.5), deps=[ml])
        mn = ctx.v(lambda e: e.tensor_scalar(stats[:, sb_ + 4:sb_ + 5], stats[:, sb_:sb_ + 1],
                                             stats[:, sb_ + 3:sb_ + 4], -1.0,
                                             ALU.mult, ALU.mult), deps=[mr, ag])
        mo = ctx.v(lambda e: e.tensor_scalar(out_ap, xap, stats[:, sb_ + 3:sb_ + 4],
                                             stats[:, sb_ + 4:sb_ + 5],
                                             ALU.mult, ALU.add), deps=[mr, mn] + deps_in)
        tk_ln[par].wrote(mo)
        return mo

    stA = ExitStack()
    sbA = lambda name, shape, dt: stA.enter_context(nc.sbuf_tensor(name, shape, dt))
    psA = lambda name, shape, dt: stA.enter_context(nc.psum_tensor(name, shape, dt))

    cmask_s = sbA("cmask_s", [128, 1920], BF16)
    hsel_s = sbA("hsel_s", [16, 4, 128], BF16)
    rhi_s = sbA("rhi_s", [16, 512], BF16)
    rlo_s = sbA("rlo_s", [16, 512], BF16)
    dens = sbA("dens", [16, 512], F32)
    recs = sbA("recs", [16, 512], F32)
    denrow = sbA("denrow", [1, 2, 512], F32)
    nw_hi = sbA("nw_hi", [128, 8, 48], BF16)
    nw_lo = sbA("nw_lo", [128, 8, 48], BF16)
    nb_hi = sbA("nb_hi", [1, 48], BF16)
    nb_lo = sbA("nb_lo", [1, 48], BF16)
    mc.append(ctx.dma(lambda e: e.dma_start(out=cmask_s[:], in_=cmask[:, :]), q="Qg"))
    MCM = mc[-1]
    mc.append(ctx.dma(lambda e: e.dma_start(
        out=hsel_s[:].rearrange("p g n -> p (g n)"), in_=hsel[:, :]), q="Qg"))
    MHS = mc[-1]
    sq_big = sbA("sq_big", [128, 2, 512], F32)
    qkT = sbA("qkT", [128, 4, T], BF16)
    kvTf = sbA("kvTf", [128, T], BF16)
    wkT_s = sbA("wkT_s", [128, 16, 128], BF16)
    qtlA = sbA("qtlA", [128, 512], BF16)
    qtlB = sbA("qtlB", [128, 512], BF16)
    vc33 = sbA("vc33", [128, NT, 528], BF16)
    rms8 = sbA("rms8", [128, NT, 16], F32)
    wo_s = sbA("wo_s", [128, 4, 1024], BF16)
    qg_s = sbA("qg_s", [128, 64], F32)
    kg_s = sbA("kg_s", [128, 32], F32)
    qkg_s = sbA("qkg_s", [128, 32], F32)
    xin = sbA("xin", [128, 2, D], F32)
    expT = sbA("expT", [128, 6, 512], BF16)
    ocnT = sbA("ocnT", [128, 4, 512], BF16)
    x2s = sbA("x2s", [128, 2, D], F32)
    x3s = sbA("x3s", [128, 2, D], F32)
    lnt = sbA("lnt", [128, 2, D], F32)
    x3Th = sbA("x3Th", [128, 2, 8, 128], BF16)
    x3Tl = sbA("x3Tl", [128, 2, 8, 128], BF16)
    wq_s = sbA("wq_s", [128, 8, 1024], BF16)
    wkv_s = sbA("wkv_s", [128, 8, 128], BF16)
    wk_s = sbA("wk_s", [128, 512], BF16)
    wv_s = sbA("wv_s", [128, 512], BF16)
    nw_f = sbA("nw_f", [128, 8, 48], F32)
    nb_f = sbA("nb_f", [1, 48], F32)
    x1bf = sbA("x1bf", [128, 2, D], BF16)
    x1T = sbA("x1T", [128, 2, 8, 128], BF16)
    kvbf = sbA("kvbf", [128, 2, 128], BF16)

    for g in range(8):
        ctx.dma(lambda e, g=g: e.dma_start(
            out=wq_s[:, g, :], in_=Wq[g * 128:(g + 1) * 128, :]), q="Qg")
        ctx.dma(lambda e, g=g: e.dma_start(
            out=wkv_s[:, g, :], in_=Wkv[g * 128:(g + 1) * 128, :]), q="Qg")
        ctx.dma(lambda e, g=g: e.dma_start(
            out=nw_f[:, g, :], in_=nwT[g * 128:(g + 1) * 128, :]), q="Qg")
    ctx.dma(lambda e: e.dma_start(out=wk_s[:], in_=Wk[:, :]), q="Qg")
    ctx.dma(lambda e: e.dma_start(out=wv_s[:], in_=Wv[:, :]), q="Qg")
    ctx.dma(lambda e: e.dma_start(out=nb_f[:], in_=nbrow[:, :]), q="Qg")
    for g in range(4):
        ctx.dma(lambda e, g=g: e.dma_start(
            out=wo_s[:, g, :], in_=WoS[g * 128:(g + 1) * 128, :]), q="Qg")
    ctx.dma(lambda e: e.dma_start(out=qg_s[:], in_=qg[:, :]), q="Qg")
    ctx.dma(lambda e: e.dma_start(out=kg_s[:], in_=kg[:, :]), q="Qg")
    ctx.dma(lambda e: e.dma_start(
        out=wkT_s[:].rearrange("p h l -> p (h l)"), in_=WkTd[:, :]), q="Qg")
    MW = ("Qg", ctx.cnt["Qg"])
    m_qkg = ctx.v(lambda e: e.tensor_tensor(qkg_s[:], qg_s[:, 0:32], kg_s[:], ALU.mult),
                  deps=[MW])
    m_vc1 = ctx.v(lambda e: e.memset(
        vc33[:].rearrange("p t (h j) -> p t h j", j=33)[:, :, :, 32], 1.0))
    mhi_c = ctx.v(lambda e: e.tensor_copy(
        nw_hi[:].rearrange("p g n -> p (g n)"), nw_f[:].rearrange("p g n -> p (g n)")),
        deps=[MW])
    mlo_c = ctx.v(lambda e: e.tensor_tensor(
        nw_lo[:].rearrange("p g n -> p (g n)"), nw_f[:].rearrange("p g n -> p (g n)"),
        nw_hi[:].rearrange("p g n -> p (g n)"), ALU.subtract), deps=[mhi_c])
    mnbh = ctx.v(lambda e: e.tensor_copy(nb_hi[:], nb_f[:]), deps=[MW])
    mnbl = ctx.v(lambda e: e.tensor_tensor(nb_lo[:], nb_f[:], nb_hi[:], ALU.subtract),
                 deps=[mnbh])
    MNW = [mhi_c, mlo_c, mnbh, mnbl, m_ones_b]

    pq = psA("pq", [128, 512], F32)
    pkvc = psA("pkvc", [128, 512], F32)
    pT = psA("pT", [128, 1024], BF16)
    pscore = psA("pscore", [128, 3, 512], F32)
    pav = psA("pav", [128, 512], F32)
    paux = psA("paux", [128, 512], F32)

    t_xin = [Trk(), Trk()]
    t_x1bf = [Trk(), Trk()]
    t_x1T = [Trk(), Trk()]
    t_kvbf = [Trk(), Trk()]
    t_pq = Trk()
    t_pkvc = Trk()
    t_pT = Trk()
    t_kvT = Trk()
    t_vc33 = Trk()
    t_qkT = Trk()
    t_rms8 = Trk()
    tk_rmsq = [Trk(), Trk()]
    tk_rmsk = [Trk(), Trk()]
    tk_sqh = [Trk(), Trk()]
    rms8m = [None] * NT

    def emit_a0_tile_gen(i):
        r = i % 2
        mload = ctx.dma(lambda e, i=i, r=r: e.dma_start(
            out=xin[:, r, :], in_=x[i * 128:(i + 1) * 128, :]), deps=t_xin[r].wdeps())
        t_xin[r].wrote(mload)
        m_x1 = layer_norm(xin[:, r, :], x1bf[:, r, :],
                          [mload] + t_x1bf[r].wdeps(), par=r)
        t_xin[r].read(m_x1)
        t_x1bf[r].wrote(m_x1)
        yield
        mtr = None
        for g in range(8):
            mtr = ctx.t(lambda e, g=g, r=r: e.transpose(
                pT[:, g * 128:(g + 1) * 128], x1bf[:, r, g * 128:(g + 1) * 128],
                idbf_s[:]),
                deps=[m_x1, mc[0]] + (t_pT.wdeps() if g == 0 else []))
        t_pT.wrote(mtr)
        m_x1T = ctx.v(lambda e, r=r: e.tensor_copy(
            x1T[:, r].rearrange("p g t -> p (g t)"), pT[:]),
            deps=[mtr] + t_x1T[r].wdeps())
        t_pT.read(m_x1T)
        t_x1bf[r].read(m_x1T)
        t_x1T[r].wrote(m_x1T)
        yield
        mq_done = [None, None]
        for half in range(2):
            mq_mm = None
            for g in range(8):
                mq_mm = ctx.t(lambda e, g=g, r=r, half=half: e.matmul(
                    pq[:], x1T[:, r, g, :],
                    wq_s[:, g, half * 512:(half + 1) * 512],
                    start=(g == 0), stop=(g == 7), skip_group_check=True),
                    deps=[m_x1T, MW] + (t_pq.wdeps() if g == 0 else []))
            t_pq.wrote(mq_mm)
            yield
            mSq = ctx.s(lambda e, r=r: e.activation(
                sq_big[:, r, :], pq[:], AF.Square),
                deps=[mq_mm] + tk_sqh[r].wdeps())
            tk_sqh[r].wrote(mSq)
            mred = ctx.v(lambda e, r=r, half=half: e.reduce_sum(
                stats[:, 16 + r * 16 + half * 8:24 + r * 16 + half * 8],
                sq_big[:, r].rearrange("p (h j) -> p h j", j=64), axis=AX.X),
                deps=[mSq] + (tk_rmsq[r].wdeps() if half == 0 else []))
            tk_sqh[r].read(mred)
            mln = ctx.s(lambda e, r=r, half=half: e.activation(
                stats[:, 16 + r * 16 + half * 8:24 + r * 16 + half * 8],
                stats[:, 16 + r * 16 + half * 8:24 + r * 16 + half * 8],
                AF.Ln, scale=1.0 / 64, bias=cb[:, 1:2]), deps=[mred, MCB])
            mrs = ctx.s(lambda e, r=r, half=half: e.activation(
                stats[:, 48 + r * 16 + half * 8:56 + r * 16 + half * 8],
                stats[:, 16 + r * 16 + half * 8:24 + r * 16 + half * 8],
                AF.Exp, scale=-0.5), deps=[mln])
            tk_rmsq[r].wrote(mrs)
            mscl = ctx.v(lambda e, r=r, half=half: e.tensor_tensor(
                sq_big[:, r, half * 256:(half + 1) * 256].rearrange(
                    "p (h j) -> p h j", j=32),
                stats[:, 48 + r * 16 + half * 8:56 + r * 16 + half * 8].rearrange(
                    "p (h o) -> p h o", o=1).broadcast_to([128, 8, 32]),
                qkg_s[:].rearrange("p (o j) -> p o j", o=1).broadcast_to([128, 8, 32]),
                ALU.mult), deps=[mrs, m_qkg] + tk_sqh[r].wdeps())
            tk_sqh[r].wrote(mscl)
            tk_rmsq[r].read(mscl)
            mq = ctx.v(lambda e, r=r, half=half: e.tensor_tensor(
                x1bf[:, r, half * 256:(half + 1) * 256].rearrange(
                    "p (h j) -> p h j", j=32),
                pq[:].rearrange("p (h j) -> p h j", j=64)[:, :, 0:32],
                sq_big[:, r, half * 256:(half + 1) * 256].rearrange(
                    "p (h j) -> p h j", j=32),
                ALU.mult), deps=[mscl] + t_x1bf[r].wdeps())
            t_x1bf[r].wrote(mq)
            t_pq.read(mq)
            tk_sqh[r].read(mq)
            mq_done[half] = mq
            yield
        mtr2 = None
        for g in range(4):
            mtr2 = ctx.t(lambda e, g=g, r=r: e.transpose(
                pT[:, g * 128:(g + 1) * 128], x1bf[:, r, g * 128:(g + 1) * 128],
                idbf_s[:]), deps=mq_done + (t_pT.wdeps() if g == 0 else []))
        t_pT.wrote(mtr2)
        m_qkT = ctx.v(lambda e, i=i: e.tensor_copy(
            qkT[:, :, i * 128:(i + 1) * 128],
            pT[:, 0:512].rearrange("p (g t) -> p g t", g=4)), deps=[mtr2])
        t_pT.read(m_qkT)
        t_x1bf[r].read(m_qkT)
        t_qkT.wrote(m_qkT)
        yield
        mkv_mm = None
        for g in range(8):
            mkv_mm = ctx.t(lambda e, g=g, r=r: e.matmul(
                pkvc[:, 0:128], x1T[:, r, g, :], wkv_s[:, g, :],
                start=(g == 0), stop=(g == 7), skip_group_check=True),
                deps=[m_x1T, MW] + (t_pkvc.wdeps() if g == 0 else []))
        t_pkvc.wrote(mkv_mm)
        mkvb = ctx.s(lambda e, r=r: e.copy(kvbf[:, r], pkvc[:, 0:128]),
                     deps=[mkv_mm] + t_kvbf[r].wdeps())
        t_kvbf[r].wrote(mkvb)
        t_pkvc.read(mkvb)
        t_x1T[r].read(mkvb)
        mkvT_t = ctx.t(lambda e, r=r: e.transpose(pT[:, 512:640], kvbf[:, r],
                                                  idbf_s[:]),
                       deps=[mkvb] + t_pT.wdeps())
        t_pT.wrote(mkvT_t)
        mkvT = ctx.v(lambda e, i=i: e.tensor_copy(
            kvTf[:, i * 128:(i + 1) * 128], pT[:, 512:640]),
            deps=[mkvT_t] + t_kvT.wdeps())
        t_pT.read(mkvT)
        t_kvbf[r].read(mkvT)
        t_kvT.wrote(mkvT)
        yield
        mk_mm = ctx.t(lambda e, i=i: e.matmul(
            pkvc[:], kvTf[:, i * 128:(i + 1) * 128], wk_s[:], start=True, stop=True,
            skip_group_check=True),
            deps=[mkvT, MW] + t_pkvc.wdeps())
        t_pkvc.wrote(mk_mm)
        t_kvT.read(mk_mm)
        mSk = ctx.s(lambda e, r=r: e.activation(sq_big[:, r], pkvc[:], AF.Square),
                    deps=[mk_mm] + tk_sqh[r].wdeps())
        tk_sqh[r].wrote(mSk)
        t_pkvc.read(mSk)
        mredk = ctx.v(lambda e, r=r: e.reduce_sum(
            stats[:, 80 + r * 16:96 + r * 16],
            sq_big[:, r].rearrange("p (h j) -> p h j", j=32),
            axis=AX.X), deps=[mSk] + tk_rmsk[r].wdeps())
        tk_sqh[r].read(mredk)
        mlnk = ctx.s(lambda e, r=r: e.activation(
            stats[:, 80 + r * 16:96 + r * 16], stats[:, 80 + r * 16:96 + r * 16],
            AF.Ln, scale=1.0 / 32, bias=cb[:, 1:2]), deps=[mredk, MCB])
        mr8 = ctx.s(lambda e, i=i, r=r: e.activation(
            rms8[:, i, :], stats[:, 80 + r * 16:96 + r * 16], AF.Exp, scale=-0.5,
            bias=cb[:, 2:3]), deps=[mlnk, MCB])
        tk_rmsk[r].wrote(mr8)
        rms8m[i] = mr8
        t_rms8.wrote(mr8)
        yield
        mvc_mm = ctx.t(lambda e, i=i: e.matmul(
            pkvc[:], kvTf[:, i * 128:(i + 1) * 128], wv_s[:], start=True,
            stop=True, skip_group_check=True),
            deps=[mkvT, MW, mSk] + t_pkvc.wdeps())
        t_pkvc.wrote(mvc_mm)
        t_kvT.read(mvc_mm)
        m_vc = ctx.v(lambda e, i=i: e.tensor_copy(
            vc33[:, i, :].rearrange("p (h j) -> p h j", j=33)[:, :, 0:32],
            pkvc[:].rearrange("p (h j) -> p h j", j=32)), deps=[mvc_mm, m_vc1])
        t_pkvc.read(m_vc)
        t_vc33.wrote(m_vc)
        a0_done[i] = ("V", ctx.cnt["V"])
        yield

    a0_gen = [None]
    a0_cool = [0, 1]

    def a0_pump(force=False):
        if a0_gen[0] is None:
            return
        if not force and a0_cool[0] > 0:
            a0_cool[0] -= 1
            return
        try:
            next(a0_gen[0])
            a0_cool[0] = a0_cool[1]
        except StopIteration:
            a0_gen[0] = None

    def a0_start(ti):
        while a0_gen[0] is not None:
            a0_pump(force=True)
        a0_gen[0] = emit_a0_tile_gen(ti)
        next(a0_gen[0])
        a0_cool[0] = a0_cool[1] + 2

    def emit_a0_tile(i):
        for _ in emit_a0_tile_gen(i):
            pass
        return a0_done[i]

    t_psc = [Trk(), Trk(), Trk()]
    t_paux = Trk()
    t_qtl = [Trk(), Trk()]
    t_pav = Trk()
    t_expb = [Trk() for _ in range(6)]
    t_dens = Trk()
    t_recs = Trk()
    t_denrow = [Trk(), Trk()]
    t_ocn = Trk()
    t_Sall = Trk()
    t_x2s = [Trk(), Trk()]
    t_x3s = [Trk(), Trk()]
    t_lnt = [Trk(), Trk()]
    t_x3Th = [Trk(), Trk()]
    t_x3Tl = [Trk(), Trk()]
    m_x2st = [None] * NT
    a0_done = [None] * NT

    for i in range(4):
        a0_done[i] = emit_a0_tile(i)

    def emit_qtilde(c, h):
        hg, sl = h // 4, h % 4
        mqt = ctx.t(lambda e, h=h, sl=sl, hg=hg, c=c: e.matmul(
            paux[:], wkT_s[sl * 32:(sl + 1) * 32, h, :],
            qkT[sl * 32:(sl + 1) * 32, hg, c * 512:(c + 1) * 512],
            start=True, stop=True, tile_position=(sl * 32, 0),
            skip_group_check=True),
            deps=t_paux.wdeps() + [MW, a0_done[4 * c + 3]])
        t_paux.wrote(mqt)
        hb = h % 2
        dst = qtlA if hb == 0 else qtlB
        mqte = ctx.v(lambda e, dst=dst: e.tensor_copy(dst[:], paux[:]),
                     deps=[mqt] + t_qtl[hb].wdeps())
        t_paux.read(mqte)
        t_qtl[hb].wrote(mqte)
        return mqte

    for c in range(NC):
        n_st = 4 * c + 4
        m_qte = emit_qtilde(c, 0)
        for h in range(16):
            hg, sl = h // 4, h % 4
            hb = h % 2
            qsrc = qtlA if hb == 0 else qtlB
            mqte = m_qte
            mav = None
            mexs = [None] * (n_st + 4)
            EXPD = 6
            if c < NC - 1 and h % 4 == 0:
                a0_cool[1] = 2 if n_st >= 8 else 1
                a0_start(4 * (c + 1) + h // 4)
            for j in range(n_st):
                a0_pump()
                ps_ = j % 3
                vs = max(0, (j - 4 * c) * 128)
                msc = ctx.t(lambda e, ps_=ps_, j=j, vs=vs, qsrc=qsrc: e.matmul(
                    pscore[:, ps_, vs:512], kvTf[:, j * 128:(j + 1) * 128],
                    qsrc[:, vs:512],
                    start=True, stop=True, skip_group_check=True),
                    deps=t_psc[ps_].wdeps() + [mqte, a0_done[j]])
                t_psc[ps_].wrote(msc)
                jb = j % EXPD
                mex = ctx.s(lambda e, ps_=ps_, j=j, h=h, jb=jb, vs=vs: e.activation(
                    expT[:, jb, vs:512], pscore[:, ps_, vs:512], AF.Exp,
                    scale=rms8[:, j, h:h + 1]),
                    deps=t_psc[ps_].rdeps() + t_expb[jb].wdeps() + [rms8m[j]])
                t_psc[ps_].read(mex)
                t_expb[jb].wrote(mex)
                p_ = j - 4 * c
                if p_ >= 0:
                    mex = ctx.v(lambda e, jb=jb, vs=vs: e.tensor_tensor(
                        expT[:, jb, vs:vs + 128], expT[:, jb, vs:vs + 128],
                        cmask_s[:, 896:1024],
                        ALU.mult), deps=[mex, MCM])
                    t_expb[jb].wrote(mex)
                mexs[j] = mex
                if j == max(1, n_st - 4) and h < 15:
                    m_qte_next = emit_qtilde(c, h + 1)
                if j % 2 == 1 and j >= 3:
                    for jp in (j - 3, j - 2):
                        jpb = jp % EXPD
                        vjp = max(0, (jp - 4 * c) * 128)
                        mv = ctx.t(lambda e, h=h, jp=jp, jpb=jpb, vjp=vjp,
                                   first=(jp == 0): e.matmul(
                            pav[0:33, vjp:512], vc33[:, jp, h * 33:h * 33 + 33],
                            expT[:, jpb, vjp:512],
                            start=first, stop=False, skip_group_check=True),
                            deps=[mexs[jp]] + (t_pav.wdeps() if jp == 0 else []))
                        t_expb[jpb].read(mv)
                        mav = mv
                        if jp == 0:
                            t_pav.wrote(mv)
            last_odd = n_st - 1 if (n_st - 1) % 2 == 1 else n_st - 2
            for jp in range(last_odd - 1, n_st):
                jpb = jp % EXPD
                vjp = max(0, (jp - 4 * c) * 128)
                mv = ctx.t(lambda e, h=h, jp=jp, jpb=jpb, vjp=vjp,
                           first=(jp == 0), last=(jp == n_st - 1): e.matmul(
                    pav[0:33, vjp:512], vc33[:, jp, h * 33:h * 33 + 33],
                    expT[:, jpb, vjp:512],
                    start=first, stop=last, skip_group_check=True),
                    deps=[mexs[jp]] + (t_pav.wdeps() if jp == 0 else []))
                t_expb[jpb].read(mv)
                mav = mv
                if jp == 0:
                    t_pav.wrote(mv)
            t_pav.wrote(mav)
            t_qtl[hb].read(mexs[n_st - 1])
            if h < 15:
                m_qte = m_qte_next
            mcp = ctx.s(lambda e, sl=sl, hg=hg: e.copy(
                ocnT[sl * 32:(sl + 1) * 32, hg, :], pav[0:32, :]),
                deps=t_pav.rdeps() + t_ocn.wdeps())
            t_ocn.wrote(mcp)
            t_pav.read(mcp)
            par = h % 2
            mdr = ctx.v(lambda e, par=par: e.tensor_copy(
                denrow[0:1, par, :], pav[32:33, :]),
                deps=t_pav.rdeps() + t_denrow[par].wdeps())
            t_pav.read(mdr)
            t_denrow[par].wrote(mdr)
            mden = ctx.dma(lambda e, h=h, par=par: e.dma_start(
                out=dens[h:h + 1, :], in_=denrow[0:1, par, :]),
                deps=[mdr] + t_dens.wdeps(), q="Qs")
            t_denrow[par].read(mden)
            t_dens.wrote(mden)
        while a0_gen[0] is not None:
            a0_pump(force=True)
        mrcp = ctx.v(lambda e: e.reciprocal(recs[:], dens[:]),
                     deps=t_dens.wdeps() + t_recs.wdeps())
        t_dens.read(mrcp)
        t_recs.wrote(mrcp)
        mhi_r = ctx.v(lambda e: e.tensor_copy(rhi_s[:], recs[:]), deps=[mrcp])
        mlo_r = ctx.v(lambda e: e.tensor_tensor(rlo_s[:], recs[:], rhi_s[:],
                                                ALU.subtract), deps=[mhi_r])
        t_recs.read(mlo_r)
        for hg in range(4):
            mbc = ctx.t(lambda e, hg=hg: e.matmul(
                pav[:], hsel_s[:, hg, :], rhi_s[:], start=True, stop=False,
                skip_group_check=True),
                deps=[mhi_r, MHS] + t_pav.wdeps())
            mbc = ctx.t(lambda e, hg=hg: e.matmul(
                pav[:], hsel_s[:, hg, :], rlo_s[:], start=False, stop=True,
                skip_group_check=True),
                deps=[mlo_r])
            t_pav.wrote(mbc)
            mno = ctx.v(lambda e, hg=hg: e.tensor_tensor(
                ocnT[:, hg, :], ocnT[:, hg, :], pav[:], ALU.mult),
                deps=[mbc] + t_ocn.wdeps())
            t_pav.read(mno)
            t_ocn.wrote(mno)
        m_ocn_all = ("V", ctx.cnt["V"])
        ep_stageB = [None]
        for tt in range(4):
            it = c * 4 + tt
            r = it % 2
            mwo = None
            for n in range(2):
                for g in range(4):
                    mwo = ctx.t(lambda e, n=n, g=g, tt=tt: e.matmul(
                        pscore[:, n, :], ocnT[:, g, tt * 128:(tt + 1) * 128],
                        wo_s[:, g, n * 512:(n + 1) * 512],
                        start=(g == 0), stop=(g == 3), skip_group_check=True),
                        deps=[m_ocn_all, MW] + (t_psc[0].wdeps() + t_psc[1].wdeps()
                                                if (n == 0 and g == 0) else []))
            t_psc[0].wrote(mwo)
            t_psc[1].wrote(mwo)
            t_ocn.read(mwo)
            if ep_stageB[0] is not None:
                ep_stageB[0]()
                ep_stageB[0] = None
            mxre = ctx.dma(lambda e, it=it, r=r: e.dma_start(
                out=xin[:, r, :], in_=x[it * 128:(it + 1) * 128, :]),
                deps=t_xin[r].wdeps())
            t_xin[r].wrote(mxre)
            a_ap = pscore[:, 0:2, :].rearrange("p s n -> p (s n)")
            mo = layer_norm(a_ap, lnt[:, r, :], [mwo] + t_lnt[r].wdeps(), par=r)
            t_lnt[r].wrote(mo)
            t_psc[0].read(mo)
            t_psc[1].read(mo)
            t_ocn.read(mo)
            mx2 = ctx.v(lambda e, r=r: e.tensor_add(x2s[:, r, :], lnt[:, r, :],
                                                    xin[:, r, :]),
                        deps=[mo, mxre] + t_x2s[r].wdeps())
            t_lnt[r].read(mx2)
            t_xin[r].read(mx2)
            t_x2s[r].wrote(mx2)
            mst2 = ctx.dma(lambda e, it=it, r=r: e.dma_start(
                out=x2h[it * 128:(it + 1) * 128, :], in_=x2s[:, r, :]), deps=[mx2])
            t_x2s[r].read(mst2)
            m_x2st[it] = mst2
            if dbg:
                md = ctx.dma(lambda e, it=it, r=r: e.dma_start(
                    out=dbg_x2[it * 128:(it + 1) * 128, :], in_=x2s[:, r, :]),
                    deps=[mx2])
                t_x2s[r].read(md)
            mo3 = layer_norm(x2s[:, r, :], x3s[:, r, :], [mx2] + t_x3s[r].wdeps(),
                             par=r)
            t_x2s[r].read(mo3)
            t_x3s[r].wrote(mo3)

            def make_stageB(it=it, r=r, mo3=mo3):
                def sb():
                    mtr3a = None
                    for g in range(4):
                        mtr3a = ctx.t(lambda e, g=g, r=r: e.transpose(
                            pq[:, g * 128:(g + 1) * 128],
                            x3s[:, r, g * 128:(g + 1) * 128], idf_s[:]),
                            deps=[mo3, mc[1]] + (t_pq.wdeps() if g == 0 else []))
                    t_pq.wrote(mtr3a)
                    mtr3b = None
                    for g in range(4, 8):
                        mtr3b = ctx.t(lambda e, g=g, r=r: e.transpose(
                            pkvc[:, (g - 4) * 128:(g - 3) * 128],
                            x3s[:, r, g * 128:(g + 1) * 128], idf_s[:]),
                            deps=[mo3, mc[1]] + (t_pkvc.wdeps() if g == 4 else []))
                    t_pkvc.wrote(mtr3b)
                    t_x3s[r].read(mtr3a)
                    t_x3s[r].read(mtr3b)
                    mhiA = ctx.s(lambda e, r=r: e.copy(
                        x3Th[:, r, 0:4].rearrange("p g t -> p (g t)"), pq[:]),
                        deps=[mtr3a] + t_x3Th[r].wdeps())
                    mhiB = ctx.s(lambda e, r=r: e.copy(
                        x3Th[:, r, 4:8].rearrange("p g t -> p (g t)"), pkvc[:]),
                        deps=[mtr3b])
                    t_x3Th[r].wrote(mhiB)
                    t_pq.read(mhiA)
                    t_pkvc.read(mhiB)
                    mloA = ctx.v(lambda e, r=r: e.tensor_tensor(
                        x3Tl[:, r, 0:4].rearrange("p g t -> p (g t)"), pq[:],
                        x3Th[:, r, 0:4].rearrange("p g t -> p (g t)"), ALU.subtract),
                        deps=[mhiA] + t_x3Tl[r].wdeps())
                    mloB = ctx.v(lambda e, r=r: e.tensor_tensor(
                        x3Tl[:, r, 4:8].rearrange("p g t -> p (g t)"), pkvc[:],
                        x3Th[:, r, 4:8].rearrange("p g t -> p (g t)"), ALU.subtract),
                        deps=[mhiB])
                    t_x3Tl[r].wrote(mloB)
                    t_pq.read(mloA)
                    t_pkvc.read(mloB)
                    mhi = [mhiA, mhiB]
                    mlo = [mloA, mloB]
                    mhst = ctx.dma(lambda e, it=it, r=r: e.dma_start(
                        out=x3hiTh.rearrange("(g p) t -> p g t", g=8)[:, :, it * 128:(it + 1) * 128],
                        in_=x3Th[:, r]), deps=mhi)
                    t_x3Th[r].read(mhst)
                    mmS = ctx.t(lambda e, r=r: e.matmul(
                        paux[:, 0:48], x3Th[:, r, 0, :], nw_hi[:, 0, :],
                        start=True, stop=False, skip_group_check=True),
                        deps=mhi + mlo + MNW + t_paux.wdeps())
                    mmS = ctx.t(lambda e: e.matmul(paux[:, 0:48], ones1b_s[:], nb_hi[:],
                                                   start=False, stop=False,
                                                   skip_group_check=True), deps=[])
                    mmS = ctx.t(lambda e: e.matmul(paux[:, 0:48], ones1b_s[:], nb_lo[:],
                                                   start=False, stop=False,
                                                   skip_group_check=True), deps=[])
                    for g in range(8):
                        mmS = ctx.t(lambda e, g=g, r=r: e.matmul(
                            paux[:, 0:48], x3Th[:, r, g, :], nw_lo[:, g, :],
                            start=False, stop=False, skip_group_check=True), deps=[])
                    for g in range(8):
                        mmS = ctx.t(lambda e, g=g, r=r: e.matmul(
                            paux[:, 0:48], x3Tl[:, r, g, :], nw_hi[:, g, :],
                            start=False, stop=False, skip_group_check=True), deps=[])
                    for g in range(1, 8):
                        mmS = ctx.t(lambda e, g=g, r=r: e.matmul(
                            paux[:, 0:48], x3Th[:, r, g, :], nw_hi[:, g, :],
                            start=False, stop=(g == 7), skip_group_check=True), deps=[])
                    t_paux.wrote(mmS)
                    mse = ctx.v(lambda e, it=it: e.tensor_copy(
                        S_all[:, it, :], paux[:, 0:48]), deps=[mmS])
                    t_paux.read(mse)
                    t_x3Th[r].read(mse)
                    t_x3Tl[r].read(mse)
                    t_Sall.wrote(mse)
                return sb
            ep_stageB[0] = make_stageB()
        if ep_stageB[0] is not None:
            ep_stageB[0]()
            ep_stageB[0] = None

    mA1_Q = ("Qs", ctx.cnt["Qs"])
    stA.close()
    ctx.barrier()

    stB = ExitStack()
    sbB = lambda name, shape, dt: stB.enter_context(nc.sbuf_tensor(name, shape, dt))
    kTf = sbB("kTf", [128, 8, T], BF16)
    vS = sbB("vS", [128, NT, 1026], BF16)
    stSEL = ExitStack()
    selT = stSEL.enter_context(nc.sbuf_tensor("selT", [16, 3, T], BF16))

    stR = ExitStack()
    sbR = lambda name, shape, dt: stR.enter_context(nc.sbuf_tensor(name, shape, dt))
    psR = lambda name, shape, dt: stR.enter_context(nc.psum_tensor(name, shape, dt))
    selt = sbR("selt", [128, NT, 48], BF16)
    routT = sbR("routT", [128, NT, 34], F32)
    pTr = psR("pTr", [16, 128], BF16)
    mS_done = ("V", ctx.cnt["V"])
    if dbg:
        dbS_s = sbR("dbS_s", [128, NT * 48], F32)
        mdbS = ctx.v(lambda e: e.tensor_copy(
            dbS_s[:], S_all[:].rearrange("p t n -> p (t n)")), deps=[mS_done])
        ctx.dma(lambda e: e.dma_start(out=dbg_S[:, :], in_=dbS_s[:]), deps=[mdbS])

    m_sel_done = []
    for fi in range(3):
        base = fi * 16
        mH = ctx.v(lambda e, base=base: e.tensor_single_scalar(
            routT[:, :, 0:15], S_all[:, :, base:base + 15], 0.0, ALU.is_ge),
            deps=[mS_done] + (m_sel_done[-1:] if m_sel_done else []))
        mHn = ctx.v(lambda e: e.tensor_scalar(
            routT[:, :, 15:30], routT[:, :, 0:15], -1.0, 1.0, ALU.mult, ALU.add),
            deps=[mH])
        d2 = [mHn]
        for n4 in range(4):
            a_, b_ = divmod(n4, 2)
            pcol = 15 if a_ == 0 else 0
            ccol = (15 + 1 + a_) if b_ == 0 else (1 + a_)
            m2 = ctx.v(lambda e, n4=n4, pcol=pcol, ccol=ccol: e.tensor_tensor(
                routT[:, :, 30 + n4], routT[:, :, pcol], routT[:, :, ccol], ALU.mult),
                deps=d2)
            d2 = [m2]
        for n8 in range(8):
            a_, b_ = divmod(n8, 2)
            node = 3 + a_
            ccol = (15 + node) if b_ == 0 else node
            m3 = ctx.v(lambda e, n8=n8, a_=a_, ccol=ccol, base=base: e.tensor_tensor(
                S_all[:, :, base + n8], routT[:, :, 30 + a_], routT[:, :, ccol],
                ALU.mult), deps=d2)
            d2 = [m3]
        for l in range(16):
            a_, b_ = divmod(l, 2)
            node = 7 + a_
            ccol = (15 + node) if b_ == 0 else node
            msl = ctx.v(lambda e, l=l, a_=a_, ccol=ccol, base=base: e.tensor_tensor(
                selt[:, :, base + l], S_all[:, :, base + a_], routT[:, :, ccol],
                ALU.mult), deps=d2)
            d2 = [msl]
        m_sel_done.append(d2[0])
    if dbg:
        dbsl = sbR("dbsl", [128, NT * 48], F32)
        mdb = ctx.v(lambda e: e.tensor_copy(
            dbsl[:], selt[:].rearrange("p t n -> p (t n)")), deps=m_sel_done)
        ctx.dma(lambda e: e.dma_start(out=dbg_sel[:, :], in_=dbsl[:]), deps=[mdb])
    t_pTr = Trk()
    m_selT = None
    for fi in range(3):
        for i in range(NT):
            mt_ = ctx.t(lambda e, fi=fi, i=i: e.transpose(
                pTr[:], selt[:, i, fi * 16:(fi + 1) * 16], idbf_s[:]),
                deps=[m_sel_done[fi]] + t_pTr.wdeps())
            t_pTr.wrote(mt_)
            m_selT = ctx.v(lambda e, fi=fi, i=i: e.tensor_copy(
                selT[:, fi, i * 128:(i + 1) * 128], pTr[:]), deps=[mt_])
            t_pTr.read(m_selT)
    mselT = ("V", ctx.cnt["V"])

    stR.close()
    ctx.barrier()

    stW1 = ExitStack()
    sbW1 = lambda name, shape, dt: stW1.enter_context(nc.sbuf_tensor(name, shape, dt))
    psW1 = lambda name, shape, dt: stW1.enter_context(nc.psum_tensor(name, shape, dt))
    w1_s = sbW1("w1_s", [128, 8, 1024], BF16)
    b1_s = sbW1("b1_s", [128, 8], F32)
    hm_s = sbW1("hm_s", [128, 8, 512], BF16)
    relu_s = sbW1("relu_s", [128, 1024], F32)
    x3c_r = sbW1("x3c_r", [128, 2, 8, 512], BF16)
    pH = psW1("pH", [128, 1024], F32)
    pM = psW1("pM", [128, 1024], F32)

    t_w1 = Trk()
    t_b1 = Trk()
    t_hm = [Trk() for _ in range(8)]
    t_relu = [Trk(), Trk()]
    t_pH = [Trk(), Trk()]
    t_pM = [Trk(), Trk()]
    t_x3c = [Trk(), Trk(), Trk()]
    m_hf_st = {f: [None] * NC for f in "qkv"}
    for fi, f in enumerate(["q", "k", "v"]):
        mw = None
        for g in range(8):
            mw = ctx.dma(lambda e, g=g, f=f: e.dma_start(
                out=w1_s[:, g, :], in_=Wff[f]["w1"][g * 128:(g + 1) * 128, :]), q="Qg",
                deps=(t_w1.wdeps() if g == 0 else []))
        mwb = ctx.dma(lambda e, f=f: e.dma_start(out=b1_s[:], in_=Wff[f]["b1"][:, :]),
                      q="Qg", deps=t_b1.wdeps())
        MWF = ("Qg", ctx.cnt["Qg"])
        t_w1.wrote(MWF)
        t_b1.wrote(MWF)
        for c in range(NC):
            rc_ = c % 2
            mx3 = ctx.dma(lambda e, c=c, rc_=rc_: e.dma_start(
                out=x3c_r[:, rc_],
                in_=x3hiTh.rearrange("(g p) t -> p g t", g=8)[:, :, c * 512:(c + 1) * 512]),
                deps=t_x3c[rc_].wdeps() + [mA1_Q])
            t_x3c[rc_].wrote(mx3)
            for m in range(8):
                pp = m % 2
                mH1 = None
                for g in range(8):
                    mH1 = ctx.t(lambda e, m=m, g=g, rc_=rc_, pp=pp: e.matmul(
                        pH[:, pp * 512:(pp + 1) * 512],
                        w1_s[:, g, m * 128:(m + 1) * 128],
                        x3c_r[:, rc_, g, :],
                        start=(g == 0), stop=(g == 7), skip_group_check=True),
                        deps=[MWF, mx3] + (t_pH[pp].wdeps() if g == 0 else []))
                t_pH[pp].wrote(mH1)
                mM1 = ctx.t(lambda e, m=m, fi=fi, c=c, pp=pp: e.matmul(
                    pM[:, pp * 512:(pp + 1) * 512],
                    esel_s[:, m * 128:(m + 1) * 128],
                    selT[:, fi, c * 512:(c + 1) * 512],
                    start=True, stop=True, skip_group_check=True),
                    deps=[mselT, MC] + t_pM[pp].wdeps())
                t_pM[pp].wrote(mM1)
                mre = ctx.s(lambda e, m=m, pp=pp: e.activation(
                    relu_s[:, pp * 512:(pp + 1) * 512], pH[:, pp * 512:(pp + 1) * 512],
                    AF.Relu, bias=b1_s[:, m:m + 1]),
                    deps=[mH1, MWF] + t_relu[pp].wdeps())
                t_relu[pp].wrote(mre)
                t_pH[pp].read(mre)
                t_w1.read(mre)
                t_b1.read(mre)
                t_x3c[rc_].read(mre)
                mhm = ctx.v(lambda e, m=m, pp=pp: e.tensor_tensor(
                    hm_s[:, m, :], relu_s[:, pp * 512:(pp + 1) * 512],
                    pM[:, pp * 512:(pp + 1) * 512], ALU.mult),
                    deps=[mre, mM1] + t_hm[m].wdeps())
                t_relu[pp].read(mhm)
                t_pM[pp].read(mhm)
                t_hm[m].wrote(mhm)
                mhv = ctx.dma(lambda e, m=m, c=c, f=f: e.dma_start(
                    out=hfh[f].rearrange("(m p) t -> p m t", m=8)[:, m, c * 512:(c + 1) * 512],
                    in_=hm_s[:, m, :]), deps=[mhm], q="Qg")
                t_hm[m].read(mhv)
                m_hf_st[f][c] = mhv
    stW1.close()
    ctx.barrier()

    stW2 = ExitStack()
    sbW2 = lambda name, shape, dt: stW2.enter_context(nc.sbuf_tensor(name, shape, dt))
    psW2 = lambda name, shape, dt: stW2.enter_context(nc.psum_tensor(name, shape, dt))
    w2_s = sbW2("w2_s", [128, 8, 1024], BF16)
    b2_s = sbW2("b2_s", [16, 1024], BF16)
    hm_r = sbW2("hm_r", [128, 2, 8, 512], BF16)
    qst = sbW2("qst", [128, 2, 512], BF16)
    pO = psW2("pO", [128, 2048], F32)

    t_w2 = Trk()
    t_b2 = Trk()
    t_hmr = [Trk(), Trk()]
    t_pO = [Trk(), Trk()]
    t_qst = [Trk(), Trk()]
    t_vS = Trk()
    t_kTf = Trk()
    m_v1s = ctx.v(lambda e: e.memset(vS[:, :, 1024], 1.0))
    for fi, f in enumerate(["q", "k", "v"]):
        mw = None
        for g in range(8):
            mw = ctx.dma(lambda e, g=g, f=f: e.dma_start(
                out=w2_s[:, g, :], in_=Wff[f]["w2"][g * 128:(g + 1) * 128, :]), q="Qg",
                deps=(t_w2.wdeps() if g == 0 else []))
        mwb2 = ctx.dma(lambda e, f=f: e.dma_start(out=b2_s[:], in_=Wff[f]["b2"][:, :]),
                       q="Qg", deps=t_b2.wdeps())
        MWF2 = ("Qg", ctx.cnt["Qg"])
        t_b2.wrote(MWF2)
        t_w2.wrote(MWF2)
        for c in range(NC):
            rc_ = c % 2
            mhl = ctx.dma(lambda e, c=c, rc_=rc_, f=f: e.dma_start(
                out=hm_r[:, rc_].rearrange("p m t -> p (m t)"),
                in_=hfh[f].rearrange("(m p) t -> p m t", m=8)[:, :, c * 512:(c + 1) * 512]),
                deps=t_hmr[rc_].wdeps() + [m_hf_st[f][c]])
            t_hmr[rc_].wrote(mhl)
            if f in ("q", "k"):
                for o in range(8):
                    po = o % 2
                    mo_ = None
                    for m in range(8):
                        mo_ = ctx.t(lambda e, o=o, m=m, rc_=rc_, po=po: e.matmul(
                            pO[:, po * 512:(po + 1) * 512],
                            w2_s[:, m, o * 128:(o + 1) * 128],
                            hm_r[:, rc_, m, :], start=(m == 0), stop=False,
                            skip_group_check=True),
                            deps=[mhl, MWF2] + (t_pO[po].wdeps() if m == 0 else []))
                    mo_ = ctx.t(lambda e, o=o, fi=fi, c=c, po=po: e.matmul(
                        pO[:, po * 512:(po + 1) * 512],
                        b2_s[:, o * 128:(o + 1) * 128],
                        selT[:, fi, c * 512:(c + 1) * 512],
                        start=False, stop=True, skip_group_check=True),
                        deps=[mwb2, mselT])
                    t_pO[po].wrote(mo_)
                    if f == "k":
                        mev = ctx.v(lambda e, o=o, c=c, po=po: e.tensor_copy(
                            kTf[:, o, c * 512:(c + 1) * 512], pO[:, po * 512:(po + 1) * 512]),
                            deps=[mo_])
                        t_kTf.wrote(mev)
                        t_hmr[rc_].read(mev)
                        t_w2.read(mev)
                        t_b2.read(mev)
                    else:
                        rq = (c * 8 + o) % 2
                        mev = ctx.v(lambda e, rq=rq, po=po: e.tensor_copy(
                            qst[:, rq, :], pO[:, po * 512:(po + 1) * 512]),
                            deps=[mo_] + t_qst[rq].wdeps())
                        t_qst[rq].wrote(mev)
                        t_hmr[rc_].read(mev)
                        t_w2.read(mev)
                        t_b2.read(mev)
                        mq_st = ctx.dma(lambda e, o=o, c=c, rq=rq: e.dma_start(
                            out=qTh[o * 128:(o + 1) * 128, c * 512:(c + 1) * 512],
                            in_=qst[:, rq, :]), deps=[mev], q="Qg")
                        t_qst[rq].read(mq_st)
                        if dbg:
                            mq2 = ctx.dma(lambda e, o=o, c=c, rq=rq: e.dma_start(
                                out=dbg_qT[o * 128:(o + 1) * 128, c * 512:(c + 1) * 512],
                                in_=qst[:, rq, :]), deps=[mev])
                            t_qst[rq].read(mq2)
                    t_pO[po].read(mev)
            else:
                for tt in range(4):
                    it = c * 4 + tt
                    pv = tt % 2
                    mv_ = None
                    for oc in range(2):
                        for m in range(8):
                            mv_ = ctx.t(lambda e, m=m, oc=oc, tt=tt, rc_=rc_, pv=pv: e.matmul(
                                pO[:, pv * 1024 + oc * 512:pv * 1024 + (oc + 1) * 512],
                                hm_r[:, rc_, m, tt * 128:(tt + 1) * 128],
                                w2_s[:, m, oc * 512:(oc + 1) * 512],
                                start=(m == 0), stop=False, skip_group_check=True),
                                deps=[mhl, MWF2] + (t_pO[pv].wdeps() if (oc == 0 and m == 0) else []))
                        mv_ = ctx.t(lambda e, oc=oc, it=it, pv=pv: e.matmul(
                            pO[:, pv * 1024 + oc * 512:pv * 1024 + (oc + 1) * 512],
                            selT[:, 2, it * 128:(it + 1) * 128],
                            b2_s[:, oc * 512:(oc + 1) * 512],
                            start=False, stop=True, skip_group_check=True),
                            deps=[mselT])
                    t_pO[pv].wrote(mv_)
                    mvev = ctx.v(lambda e, it=it, pv=pv: e.tensor_copy(
                        vS[:, it, 0:1024], pO[:, pv * 1024:(pv + 1) * 1024]),
                        deps=[mv_, m_v1s])
                    t_pO[pv].read(mvev)
                    t_hmr[rc_].read(mvev)
                    t_w2.read(mvev)
                    t_b2.read(mvev)
                    t_vS.wrote(mvev)
    MQST = ("Qg", ctx.cnt["Qg"])
    MVS = ("V", ctx.cnt["V"])
    if dbg:
        for i in range(NT):
            ctx.dma(lambda e, i=i: e.dma_start(
                out=dbg_v[:, i * 1024:(i + 1) * 1024], in_=vS[:, i, 0:1024]),
                deps=[MVS])
        for g in range(8):
            ctx.dma(lambda e, g=g: e.dma_start(
                out=dbg_kTb[g * 128:(g + 1) * 128, :], in_=kTf[:, g, :]),
                deps=[t_kTf.w, MVS])
    stW2.close()
    stSEL.close()
    ctx.barrier()

    stF = ExitStack()
    sbF = lambda name, shape, dt: stF.enter_context(nc.sbuf_tensor(name, shape, dt))
    psF = lambda name, shape, dt: stF.enter_context(nc.psum_tensor(name, shape, dt))
    qTc = sbF("qTc", [128, 2, 8, 512], BF16)
    expc = sbF("expc", [128, NT, 512], BF16)
    f_s = sbF("f_s", [128, 1024], F32)
    outs = sbF("outs", [128, 2, 1024], F32)
    x2r = sbF("x2r", [128, 2, 1024], F32)
    recf = sbF("recf", [128, 1], F32)
    pSc = psF("pSc", [128, 1024], F32)
    pF = psF("pF", [128, 2048], F32)
    pDen = psF("pDen", [128, 16], F32)
    t_qTc = [Trk(), Trk()]
    t_expc = [Trk() for _ in range(NT)]
    t_pScb = [Trk(), Trk()]
    t_pF = [Trk(), Trk()]
    t_pDen = [Trk(), Trk()]
    t_fs = Trk()
    t_outs = [Trk(), Trk()]
    t_x2r = [Trk(), Trk()]
    t_recf = Trk()

    m_final = None
    for c in range(NC):
        r = c % 2
        mql = ctx.dma(lambda e, c=c, r=r: e.dma_start(
            out=qTc[:, r].rearrange("p g t -> p (g t)"),
            in_=qTh.rearrange("(g p) t -> p g t", g=8)[:, :, c * 512:(c + 1) * 512]),
            deps=t_qTc[r].wdeps() + [MQST])
        t_qTc[r].wrote(mql)
        for j in range(NT):
            jr = j % 2
            msc = None
            for g in range(8):
                msc = ctx.t(lambda e, g=g, j=j, r=r, jr=jr: e.matmul(
                    pSc[:, jr * 512:(jr + 1) * 512], kTf[:, g, j * 128:(j + 1) * 128],
                    qTc[:, r, g, :], start=(g == 0), stop=(g == 7),
                    skip_group_check=True),
                    deps=[mql, t_kTf.w, MVS] + (t_pScb[jr].wdeps() if g == 0 else []))
            t_pScb[jr].wrote(msc)
            mex = ctx.s(lambda e, j=j, jr=jr: e.activation(
                expc[:, j, :], pSc[:, jr * 512:(jr + 1) * 512], AF.Exp,
                scale=1.0 / 32.0), deps=[msc] + t_expc[j].wdeps())
            t_pScb[jr].read(mex)
            t_qTc[r].read(mex)
            t_expc[j].wrote(mex)
        for tt in range(4):
            it = c * 4 + tt
            rr = it % 2
            pf = tt % 2
            mf_ = None
            md_ = None
            for j in range(NT):
                mf_ = ctx.t(lambda e, j=j, tt=tt, pf=pf: e.matmul(
                    pF[:, pf * 1024:pf * 1024 + 512], expc[:, j, tt * 128:(tt + 1) * 128],
                    vS[:, j, 0:512], start=(j == 0), stop=(j == NT - 1),
                    skip_group_check=True),
                    deps=[t_expc[j].w, MVS] + (t_pF[pf].wdeps() if j == 0 else []))
                mf_ = ctx.t(lambda e, j=j, tt=tt, pf=pf: e.matmul(
                    pF[:, pf * 1024 + 512:pf * 1024 + 1024],
                    expc[:, j, tt * 128:(tt + 1) * 128],
                    vS[:, j, 512:1024], start=(j == 0), stop=(j == NT - 1),
                    skip_group_check=True), deps=[])
                md_ = ctx.t(lambda e, j=j, tt=tt, pf=pf: e.matmul(
                    pDen[:, pf * 8:pf * 8 + 1], expc[:, j, tt * 128:(tt + 1) * 128],
                    vS[:, j, 1024:1025], start=(j == 0), stop=(j == NT - 1),
                    skip_group_check=True),
                    deps=(t_pDen[pf].wdeps() if j == 0 else []))

            t_pF[pf].wrote(mf_)
            t_pDen[pf].wrote(md_)
            mrc = ctx.v(lambda e, pf=pf: e.reciprocal(recf[:], pDen[:, pf * 8:pf * 8 + 1]),
                        deps=[md_] + t_recf.wdeps())
            t_pDen[pf].read(mrc)
            t_recf.wrote(mrc)
            mfs = ctx.v(lambda e, pf=pf: e.tensor_scalar(
                f_s[:], pF[:, pf * 1024:(pf + 1) * 1024], recf[:], None, ALU.mult),
                deps=[mf_, mrc] + t_fs.wdeps())
            t_pF[pf].read(mfs)
            t_recf.read(mfs)
            t_fs.wrote(mfs)
            if tt == 3:
                for j in range(NT):
                    t_expc[j].read(mfs)
            mx2l = ctx.dma(lambda e, it=it, rr=rr: e.dma_start(
                out=x2r[:, rr, :], in_=x2h[it * 128:(it + 1) * 128, :]),
                deps=t_x2r[rr].wdeps() + [m_x2st[it]])
            t_x2r[rr].wrote(mx2l)
            mo_ = layer_norm(f_s[:], outs[:, rr, :], [mfs] + t_outs[rr].wdeps(), par=rr)
            t_fs.read(mo_)
            mof = ctx.v(lambda e, rr=rr: e.tensor_add(outs[:, rr, :], outs[:, rr, :],
                                                      x2r[:, rr, :]),
                        deps=[mo_, mx2l])
            t_x2r[rr].read(mof)
            t_outs[rr].wrote(mof)
            m_final = ctx.dma(lambda e, it=it, rr=rr: e.dma_start(
                out=out[it * 128:(it + 1) * 128, :], in_=outs[:, rr, :]), deps=[mof],
                q="Qg")
            t_outs[rr].read(m_final)

    ctx.finalize([("Qs", ctx.cnt["Qs"]), ("Qg", ctx.cnt["Qg"])])
    stF.close()
    stB.close()
    st.close()
    return nc


def make_inputs(x_shard, W):
    import ml_dtypes
    T = x_shard.shape[0]
    wo_rows = np.concatenate([W["L0_attn_Wo"][h * 64:h * 64 + 32, :] for h in range(16)],
                             axis=0)
    nwT = np.zeros((1024, 48), dtype=np.float32)
    nb = np.zeros((1, 48), dtype=np.float32)
    for fi, f in enumerate("qkv"):
        nwT[:, fi * 16:fi * 16 + 15] = np.asarray(W[f"L0_ffn_{f}_node_w"]).T
        nb[0, fi * 16:fi * 16 + 15] = np.asarray(W[f"L0_ffn_{f}_node_b"])
    E = np.zeros((16, 1024), dtype=np.float32)
    for m in range(8):
        E[2 * m, m * 128:m * 128 + 64] = 1.0
        E[2 * m + 1, m * 128 + 64:m * 128 + 128] = 1.0
    M = (np.arange(1920)[None, :] - 896 >= np.arange(128)[:, None]).astype(np.float32)
    hs = np.zeros((16, 4, 128), dtype=np.float32)
    for hg in range(4):
        for p in range(128):
            hs[4 * hg + p // 32, hg, p] = 1.0
    WkT = np.asarray(W["L0_attn_Wk"]).reshape(128, 16, 32).transpose(2, 1, 0)
    WkT = np.ascontiguousarray(np.tile(WkT, (4, 1, 1)))
    d = {
        "x": np.ascontiguousarray(x_shard),
        "WkT": WkT,
        "Wq": W["L0_attn_Wq"], "Wkv": W["L0_attn_Wkv"], "Wk": W["L0_attn_Wk"],
        "Wv": W["L0_attn_Wv"], "WoS": np.ascontiguousarray(wo_rows),
        "qg": np.ascontiguousarray(np.tile(np.asarray(W["L0_attn_q_g"])[None, :], (128, 1))),
        "kg": np.ascontiguousarray(np.tile(np.asarray(W["L0_attn_k_g"])[None, :], (128, 1))),
        "nwT": np.ascontiguousarray(nwT), "nbrow": nb,
        "idbf": np.eye(128, dtype=np.float32).astype(ml_dtypes.bfloat16),
        "idf": np.eye(128, dtype=np.float32),
        "ones1f": np.ones((1, 128), dtype=np.float32),
        "esel": E.astype(ml_dtypes.bfloat16),
        "cmask": M.astype(ml_dtypes.bfloat16),
        "hsel": np.ascontiguousarray(hs.reshape(16, 512)).astype(ml_dtypes.bfloat16),
    }
    for f in "qkv":
        d[f"W1{f}"] = np.ascontiguousarray(
            np.asarray(W[f"L0_ffn_{f}_w1"]).transpose(1, 0, 2).reshape(1024, 1024))
        d[f"W2{f}"] = np.ascontiguousarray(np.asarray(W[f"L0_ffn_{f}_w2"]).reshape(1024, 1024))
        d[f"b1{f}"] = np.ascontiguousarray(
            np.asarray(W[f"L0_ffn_{f}_b1"]).reshape(1024).reshape(8, 128).T)
        d[f"b2{f}"] = np.ascontiguousarray(np.asarray(W[f"L0_ffn_{f}_b2"]))
    return d


_NC_CACHE = {}


def kernel(**inputs):
    T = 4096
    if "prog" not in _NC_CACHE:
        _NC_CACHE["prog"] = build(T, dbg=False)
    nc = _NC_CACHE["prog"]
    x = np.asarray(inputs["L0_attn_x"], dtype=np.float32)
    B = x.shape[0]
    base = make_inputs(np.ascontiguousarray(x[0]), inputs)
    in_maps = []
    for b in range(B):
        m = dict(base)
        m["x"] = np.ascontiguousarray(x[b])
        in_maps.append(m)
    from concourse.bass_utils import run_bass_kernel_spmd
    res = run_bass_kernel_spmd(nc, in_maps, core_ids=list(range(B)))
    return np.stack([r["out"] for r in res.results], axis=0)
